# version 15
# baseline (speedup 1.0000x reference)
import numpy as np
from contextlib import ExitStack, contextmanager

import concourse.bass as bass
import concourse.mybir as mybir
from concourse.bass_utils import run_bass_kernel_spmd

"""Dependency-tracked raw-Bass emission helpers (hand-rolled, no Tile).

Every engine gets one progress semaphore. Every tracked instruction increments
its engine's semaphore by 1 (DMA: stream semaphores, +16). Buffers (Buf) track
last-writer and reader tokens; emission inserts the minimal wait_ge set for
RAW/WAR/WAW edges, deduplicated via per-consumer watermarks.

PE matmul accumulation groups get a single inc on the last matmul (PE completes
in program order, so that inc covers the whole group).
"""


F32 = mybir.dt.float32
BF16 = mybir.dt.bfloat16
AF = mybir.ActivationFunctionType
ALU = mybir.AluOpType
AX = mybir.AxisListType

SECTION = {"pe": "tensor", "act": "scalar", "dve": "vector", "pool": "gpsimd", "sp": "sync"}


class Buf:
    __slots__ = ("name", "t", "last_w", "readers")

    def __init__(self, name, t=None):
        self.name = name
        self.t = t
        self.last_w = {}
        self.readers = []


class Ctx:
    def __init__(self, nc):
        self.nc = nc
        self.stack = ExitStack()
        self.block = None
        self.sems = {}
        self.cnt = {}
        self.wm = {e: {} for e in SECTION}

    def __enter__(self):
        self.stack.__enter__()
        self.sem_stack = self.stack
        for e in SECTION:
            self._mk_sem(e)
        self.block = self.stack.enter_context(self.nc.Block())
        return self

    def __exit__(self, *a):
        return self.stack.__exit__(*a)

    def _mk_sem(self, key):
        stack = getattr(self, "sem_stack", self.stack)
        self.sems[key] = stack.enter_context(self.nc.semaphore("s_" + key))
        self.cnt[key] = 0

    def stream(self, key):
        if key not in self.sems:
            self._mk_sem(key)
        return key

    def sbuf(self, name, shape, dtype=F32):
        t = self.stack.enter_context(self.nc.sbuf_tensor(name, list(shape), dtype))
        return Buf(name, t)

    def psum(self, name, shape, dtype=F32):
        t = self.stack.enter_context(self.nc.psum_tensor(name, list(shape), dtype))
        return Buf(name, t)

    def dram(self, name, shape, dtype=F32, kind="Internal"):
        t = self.nc.dram_tensor(name, list(shape), dtype, kind=kind)
        return Buf(name, t)

    def _needs(self, reads, writes):
        toks = {}

        def add(tok):
            if tok is None:
                return
            k, c = tok
            if toks.get(k, 0) < c:
                toks[k] = c

        for b in reads:
            for kc in b.last_w.items():
                add(kc)
        for b in writes:
            for kc in b.last_w.items():
                add(kc)
            for t in b.readers:
                add(t)
        return toks

    def _emit_waits(self, eng, e, toks):
        wm = self.wm[eng]
        for k in sorted(toks):
            c = toks[k]
            if wm.get(k, 0) < c:
                if not (eng == "pe" and k == "pe"):
                    e.wait_ge(self.sems[k], c)
                wm[k] = c

    def _update(self, reads, writes, tok):
        k, c = tok
        for b in reads:
            b.readers.append(tok)
        for b in writes:
            if b.last_w.get(k, 0) < c:
                b.last_w[k] = c
            b.readers = []

    def fence_stream(self, skey, written=(), read=()):
        tot = self.cnt.get(skey, 0)
        for b in written:
            b.last_w = {skey: tot}
        for b in read:
            b.readers = [(skey, tot)]

    def op(self, eng, fn, reads=(), writes=(), extra_waits=()):
        toks = self._needs(reads, writes)
        for tok in extra_waits:
            k, c = tok
            if toks.get(k, 0) < c:
                toks[k] = c
        sem = self.sems[eng]

        def body(e):
            self._emit_waits(eng, e, toks)
            ins = fn(e)
            ins.then_inc(sem, 1)

        getattr(self.block, SECTION[eng])(body)
        self.cnt[eng] += 1
        self._update(reads, writes, (eng, self.cnt[eng]))
        return (eng, self.cnt[eng])

    def group(self, eng, fn, reads=(), writes=(), extra_waits=()):
        toks = self._needs(reads, writes)
        for tok in extra_waits:
            k, c = tok
            if toks.get(k, 0) < c:
                toks[k] = c
        sem = self.sems[eng]

        def body(e):
            self._emit_waits(eng, e, toks)
            last = fn(e)
            last.then_inc(sem, 1)

        getattr(self.block, SECTION[eng])(body)
        self.cnt[eng] += 1
        self._update(reads, writes, (eng, self.cnt[eng]))
        return (eng, self.cnt[eng])

    def dma(self, qeng, skey, out_ap, in_ap, reads=(), writes=(), transpose=False, **kw):
        self.stream(skey)
        toks = self._needs(reads, writes)
        sem = self.sems[skey]

        def body(e):
            self._emit_waits(qeng, e, toks)
            if transpose:
                ins = e.dma_start_transpose(out_ap, in_ap)
            else:
                ins = e.dma_start(out=out_ap, in_=in_ap, **kw)
            ins.then_inc(sem, 16)

        getattr(self.block, SECTION[qeng])(body)
        self.cnt[skey] += 16
        self._update(reads, writes, (skey, self.cnt[skey]))
        return (skey, self.cnt[skey])

    def cc(self, skey, fn, reads=(), writes=()):
        self.stream(skey)
        toks = self._needs(reads, writes)
        sem = self.sems[skey]

        def body(e):
            self._emit_waits("pool", e, toks)
            ins = fn(e)
            ins.then_inc(sem, 16)

        self.block.gpsimd(body)
        self.cnt[skey] += 16
        self._update(reads, writes, (skey, self.cnt[skey]))
        return (skey, self.cnt[skey])

    def raw(self, eng, fn):
        getattr(self.block, SECTION[eng])(fn)

    def act_table(self, idx):
        def body(e):
            e.add_instruction(mybir.InstLoadActFuncSet(
                name=self.nc.get_next_instruction_name(),
                act_func_set_id=idx, ins=[], outs=[]))
        self.raw("act", body)

    def wait_all(self, eng, toks):
        def body(e):
            self._emit_waits(eng, e, dict(toks))
        getattr(self.block, SECTION[eng])(body)

    def barrier_token(self, eng):
        return (eng, self.cnt[eng])

    def barrier(self):
        toks = {k: c for k, c in self.cnt.items() if c > 0}
        for eng in SECTION:
            self.wait_all(eng, toks.items())

    from contextlib import contextmanager

    @contextmanager
    def phase(self, name):
        from contextlib import ExitStack
        outer = self.stack
        inner = ExitStack()
        self.stack = inner
        try:
            yield
        finally:
            self.stack = outer
            self.barrier()
            inner.close()


"""Mamba-2 (SSD) single-layer emitter for raw Bass. CT layout, CS=256 chunks.

h_out = h_in + out_proj(ssd_block(rmsnorm(h_in) * norm_w))

Conventions:
  - ncum = NEGATED running cumsum of dA (positive, grows); cum = -ncum.
  - dt_s / nc_s: (32, T) fp32 band tensors (dt and ncum).
  - per-chunk transposed cols: A2 dcwT (128,128): col half*64 + h = W,
    half*64 + 32 + h = dec; B dcwTB: col half*64 + h = dt, +32+h = ncum.
"""


class MCfg:
    T = 4096
    D = 1024
    DI = 2048
    H = 32
    CS = 256
    NCH = T // CS
    NTG = T // 512
    KD = D // 128
    MI = 34
    EPS = 1e-5


def emit_mamba_layer(C: Ctx, cfg, pfx, W, io, sin_provider=None,
                     pre_hook=None, mid_hook=None):
    T, CS, NCH, H, KD, MI = cfg.T, cfg.CS, cfg.NCH, cfg.H, cfg.KD, cfg.MI
    if pre_hook is not None:
        pre_hook(C)

    dt_s = C.sbuf(pfx + "dt_s", [32, T], F32)
    nc_s = C.sbuf(pfx + "nc_s", [32, T], F32)
    b_ct = C.sbuf(pfx + "b_ct", [64, T], BF16)
    c0_ct = C.sbuf(pfx + "c0_ct", [64, T], BF16)
    nclprevT = C.sbuf(pfx + "nclprevT", [128, 512], F32)
    carry = C.sbuf(pfx + "carry", [32, 1], F32)
    negA = C.sbuf(pfx + "negA", [32, 1], F32)
    dtb = C.sbuf(pfx + "dtb", [32, 1], F32)
    convw = C.sbuf(pfx + "convw", [128, 17 * 4], F32)
    convb = C.sbuf(pfx + "convb", [128, 17], F32)
    dexp = C.sbuf(pfx + "dexp", [128, 16], F32)
    rmsw = C.sbuf(pfx + "rmsw", [128, 16], F32)
    normw = C.sbuf(pfx + "normw", [128, 8], F32)
    ident = C.sbuf(pfx + "ident", [128, 128], BF16)
    identf = C.sbuf(pfx + "identf", [128, 128], F32)
    ones1f = C.sbuf(pfx + "ones1f", [1, 128], F32)
    onescol = C.sbuf(pfx + "onescol", [128, 1], F32)
    cstc = C.sbuf(pfx + "cstc", [128, 2], F32)
    cl_dram = C.dram(pfx + "cl_dram", [32, 16], F32)

    for nm, bb in [("dt_bias", dtb), ("conv_w", convw), ("conv_b", convb),
                   ("Dexp", dexp), ("rms_w", rmsw), ("norm_w", normw), ("identf", identf)]:
        C.dma("sp", "cst", bb.t[:], W[nm][:, :], writes=[bb])
    C.dma("pool", "cst2", ident.t[:], W["ident"][:, :], writes=[ident])
    C.fence_stream("cst", written=[dtb, convw, convb, dexp, rmsw, normw, identf])
    C.fence_stream("cst2", written=[ident])
    C.op("dve", lambda e: e.memset(ones1f.t[:], 1.0), writes=[ones1f])
    C.op("dve", lambda e: e.memset(onescol.t[:], 1.0), writes=[onescol])
    C.op("dve", lambda e: e.memset(cstc.t[:, 0:1], cfg.EPS), writes=[cstc])
    C.op("dve", lambda e: e.memset(cstc.t[:, 1:2], 1.0), writes=[cstc])
    algb = C.sbuf(pfx + "algb", [32, 1], F32)
    C.dma("sp", "cst3", algb.t[:], W["A_log"][:, :], writes=[algb])
    C.act_table(6)
    C.op("act", lambda e: e.activation(out=negA.t[:], in_=algb.t[:], func=AF.Exp),
         reads=[algb], writes=[negA])
    C.op("dve", lambda e: e.memset(carry.t[:], 0.0), writes=[carry])

    sz_d, xbc_d = io["sz_d"], io["xbc_d"]
    xct_d, xtn_d, s_d = io["xct_d"], io["xtn_d"], io["s_d"]
    h_in, h_out, halo = io["h_in"], io["h_out"], io["halo"]
    sz_db, xbc_db = Buf(pfx + "sz_db"), Buf(pfx + "xbc_db")
    xct_db = [Buf(pfx + f"xct_db{c}") for c in range(NCH)]
    xtn_db = [Buf(pfx + f"xtn_db{c}") for c in range(NCH)]
    s_db = [Buf(pfx + f"s_db{c}") for c in range(NCH)]
    cl_db = Buf(pfx + "cl_db")

    hout_db = [[Buf(pfx + f"hout_db{c}_{m}") for m in range(8)] for c in range(NCH)]
    _all_hout = [b for row in hout_db for b in row]
    for cq in range(4):
        C.dma("pool", "hcp", h_out[:, cq * (T // 4):(cq + 1) * (T // 4)],
              h_in[:, cq * (T // 4):(cq + 1) * (T // 4)], writes=_all_hout[:1])
    C.fence_stream("hcp", written=_all_hout)

    with C.phase(pfx + "A1"):
        PP = [C.psum(pfx + f"ppA{i}", [128, 512], F32) for i in range(4)]
        PA = [C.psum(pfx + f"ppC{i}", [128, 512], F32) for i in range(2)]
        PROT = [PP[0], PP[1], PA[0], PA[1]]
        w_in = [C.sbuf(pfx + f"win{k}", [128, 4256], BF16) for k in range(KD)]
        for k in range(KD):
            C.dma("pool", "winld", w_in[k].t[:],
                  W["w_inT"][k * 128:(k + 1) * 128, :], writes=[w_in[k]])
        C.fence_stream("winld", written=w_in)

        hbuf = [[C.sbuf(pfx + f"hb{r}_{k}", [128, 512], F32) for k in range(KD)] for r in range(2)]
        hsq = [C.sbuf(pfx + f"hsq{k}", [128, 512], F32) for k in range(KD)]
        vbf = [[C.sbuf(pfx + f"vb{r}_{k}", [128, 512], BF16) for k in range(KD)] for r in range(2)]
        rstd = C.sbuf(pfx + "rstd", [1, 512], F32)
        zs = [C.sbuf(pfx + f"zs{q}", [128, 512], BF16) for q in range(4)]
        xs = [C.sbuf(pfx + f"xs{q}", [128, 512], BF16) for q in range(4)]
        sp1 = [C.sbuf(pfx + f"sp1_{i}", [32, 512], F32) for i in range(3)]
        nda = C.sbuf(pfx + "nda", [32, 512], F32)
        zero32 = C.sbuf(pfx + "zero32", [32, 512], F32)
        C.op("dve", lambda e: e.memset(zero32.t[:], 0.0), writes=[zero32])

        hh = [C.sbuf(pfx + f"hh{k}", [128, 4], F32) for k in range(KD)]
        vh = [C.sbuf(pfx + f"vh{k}", [128, 4], BF16) for k in range(KD)]
        hhs = C.sbuf(pfx + "hhs", [128, 4], F32)
        rsth = C.sbuf(pfx + "rsth", [1, 4], F32)
        xh = C.sbuf(pfx + "xh", [128, 4], BF16)
        for g in range(cfg.NTG):
            if mid_hook is not None:
                mid_hook(C, g)
            r = g % 2
            t0 = g * 512
            for k in range(KD):
                C.dma("pool", f"hld{r}", hbuf[r][k].t[:],
                      h_in[k * 128:(k + 1) * 128, t0:t0 + 512], writes=[hbuf[r][k]])
            C.fence_stream(f"hld{r}", written=hbuf[r])
            for k in range(KD):
                C.op("act", lambda e, k=k, r=r: e.activation(out=hsq[k].t[:], in_=hbuf[r][k].t[:],
                                                             func=AF.Square),
                     reads=[hbuf[r][k]], writes=[hsq[k]])
                C.op("pe", lambda e, k=k: e.matmul(PP[2].t[0:1, :], onescol.t[:], hsq[k].t[:],
                                                   start=(k == 0), stop=(k == KD - 1)),
                     reads=[hsq[k], onescol], writes=[PP[2]])
            C.op("act", lambda e: e.activation(out=rstd.t[:], in_=PP[2].t[0:1, 0:512], func=AF.Ln,
                                               scale=1.0 / cfg.D, bias=cstc.t[0:1, 0:1]),
                 reads=[PP[2], cstc], writes=[rstd])
            C.op("act", lambda e: e.activation(out=rstd.t[:], in_=rstd.t[:], func=AF.Exp,
                                               scale=-0.5),
                 reads=[rstd], writes=[rstd])
            C.op("pe", lambda e: e.matmul(PP[2].t[:, :], ones1f.t[:], rstd.t[:],
                                          start=True, stop=True),
                 reads=[rstd, ones1f], writes=[PP[2]])
            for k in range(KD):
                C.op("dve", lambda e, k=k, r=r: e.scalar_tensor_tensor(
                    out=vbf[r][k].t[:], in0=hbuf[r][k].t[:], scalar=normw.t[:, k:k + 1],
                    in1=PP[2].t[:, :], op0=ALU.mult, op1=ALU.mult),
                    reads=[hbuf[r][k], normw, PP[2]], writes=[vbf[r][k]])
            for m in range(MI):
                pp = PROT[m % 4]
                rows = 128 if m < 33 else 32

                def pgrp(e, m=m, r=r, pp=pp, rows=rows):
                    last = None
                    for k in range(KD):
                        last = e.matmul(pp.t[0:rows, 0:512],
                                        w_in[k].t[:, m * 128:m * 128 + rows],
                                        vbf[r][k].t[:],
                                        start=(k == 0), stop=(k == KD - 1))
                    return last
                C.group("pe", pgrp, reads=w_in + vbf[r], writes=[pp])

                if m < 16:
                    if m == 0:
                        C.act_table(18)
                    q = m % 4
                    C.op("act", lambda e, q=q, pp=pp: e.activation(
                        out=zs[q].t[:], in_=pp.t[:, 0:512], func=AF.Silu),
                        reads=[pp], writes=[zs[q]])
                    dst = sz_d[2 * g:2 * g + 2, :, m * CS:(m + 1) * CS].rearrange("c p t -> p c t")
                    src = zs[q].t[:].rearrange("p (c t) -> p c t", c=2)
                    C.dma("sp", f"szo{q}", dst, src, reads=[zs[q]], writes=[sz_db])
                elif m < 33:
                    q = m % 4
                    C.op("act", lambda e, q=q, pp=pp: e.activation(
                        out=xs[q].t[:], in_=pp.t[:, 0:512], func=AF.Copy),
                        reads=[pp], writes=[xs[q]])
                    C.dma("act", f"xbo{q}", xbc_d[m - 16, :, 3 + t0:3 + t0 + 512],
                          xs[q].t[:], reads=[xs[q]], writes=[xbc_db])
                else:
                    C.act_table(6)
                    C.op("act", lambda e, pp=pp: e.activation(
                        out=sp1[0].t[:], in_=pp.t[0:32, 0:512], func=AF.Abs, bias=dtb.t[:]),
                        reads=[pp, dtb], writes=[sp1[0]])
                    C.op("act", lambda e: e.activation(
                        out=sp1[1].t[:], in_=sp1[0].t[:], func=AF.Exp, scale=-1.0),
                        reads=[sp1[0]], writes=[sp1[1]])
                    C.op("act", lambda e: e.activation(
                        out=sp1[1].t[:], in_=sp1[1].t[:], func=AF.Ln,
                        bias=cstc.t[0:32, 1:2]),
                        reads=[sp1[1], cstc], writes=[sp1[1]])
                    C.op("act", lambda e, pp=pp: e.activation(
                        out=sp1[2].t[:], in_=pp.t[0:32, 0:512], func=AF.Relu, bias=dtb.t[:]),
                        reads=[pp, dtb], writes=[sp1[2]])
                    C.op("dve", lambda e, t0=t0: e.tensor_add(
                        dt_s.t[:, t0:t0 + 512], sp1[1].t[:], sp1[2].t[:]),
                        reads=[sp1[1], sp1[2]], writes=[dt_s])
                    C.op("dve", lambda e, t0=t0: e.tensor_scalar_mul(
                        nda.t[:], dt_s.t[:, t0:t0 + 512], negA.t[:]),
                        reads=[dt_s, negA], writes=[nda])
                    C.op("dve", lambda e, t0=t0: e.tensor_tensor_scan(
                        out=nc_s.t[:, t0:t0 + 512], data0=nda.t[:], data1=zero32.t[:],
                        initial=carry.t[:], op0=ALU.add, op1=ALU.add),
                        reads=[nda, zero32, carry], writes=[nc_s])
                    C.op("dve", lambda e, t0=t0: e.tensor_copy(
                        carry.t[:], nc_s.t[:, t0 + 511:t0 + 512]),
                        reads=[nc_s], writes=[carry])

        halo_rd = [io["halo_db"]] if "halo_db" in io else []
        for k in range(KD):
            C.dma("sp", "hl", hh[k].t[:, 0:3], halo[k * 128:(k + 1) * 128, :],
                  reads=halo_rd, writes=[hh[k]])
        C.fence_stream("hl", written=hh)
        for k in range(KD):
            C.op("act", lambda e, k=k: e.activation(out=hhs.t[:, 0:3], in_=hh[k].t[:, 0:3],
                                                    func=AF.Square),
                 reads=[hh[k]], writes=[hhs])
            C.op("pe", lambda e, k=k: e.matmul(PP[2].t[0:1, 0:3], onescol.t[:], hhs.t[:, 0:3],
                                               start=(k == 0), stop=(k == KD - 1)),
                 reads=[hhs, onescol], writes=[PP[2]])
        C.op("act", lambda e: e.activation(out=rsth.t[:, 0:3], in_=PP[2].t[0:1, 0:3], func=AF.Ln,
                                           scale=1.0 / cfg.D, bias=cstc.t[0:1, 0:1]),
             reads=[PP[2], cstc], writes=[rsth])
        C.op("act", lambda e: e.activation(out=rsth.t[:, 0:3], in_=rsth.t[:, 0:3], func=AF.Exp,
                                           scale=-0.5),
             reads=[rsth], writes=[rsth])
        C.op("pe", lambda e: e.matmul(PP[2].t[:, 0:3], ones1f.t[:], rsth.t[:, 0:3],
                                      start=True, stop=True),
             reads=[rsth, ones1f], writes=[PP[2]])
        for k in range(KD):
            C.op("dve", lambda e, k=k: e.scalar_tensor_tensor(
                out=vh[k].t[:, 0:3], in0=hh[k].t[:, 0:3], scalar=normw.t[:, k:k + 1],
                in1=PP[2].t[:, 0:3], op0=ALU.mult, op1=ALU.mult),
                reads=[hh[k], normw, PP[2]], writes=[vh[k]])
        for m in range(16, 33):
            def hgrp(e, m=m):
                last = None
                for k in range(KD):
                    last = e.matmul(PP[3].t[:, 0:3],
                                    w_in[k].t[:, m * 128:(m + 1) * 128],
                                    vh[k].t[:, 0:3],
                                    start=(k == 0), stop=(k == KD - 1))
                return last
            C.group("pe", hgrp, reads=w_in + vh, writes=[PP[3]])
            C.op("act", lambda e: e.activation(out=xh.t[:, 0:3], in_=PP[3].t[:, 0:3],
                                               func=AF.Copy),
                 reads=[PP[3]], writes=[xh])
            C.dma("sp", "hxo", xbc_d[m - 16, :, 0:3], xh.t[:, 0:3], reads=[xh],
                  writes=[xbc_db])

        cl_sb = C.sbuf(pfx + "cl_sb", [32, 16], F32)
        src = nc_s.t[:].rearrange("p (c t) -> p c t", c=NCH)[:, :, CS - 1:CS]
        C.op("dve", lambda e: e.tensor_copy(cl_sb.t[:, :], src), reads=[nc_s], writes=[cl_sb])
        C.dma("sp", "clo", cl_dram.t[:, :], cl_sb.t[:], reads=[cl_sb], writes=[cl_db])
        nclrow = C.sbuf(pfx + "nclrow", [1, 512], F32)
        C.op("dve", lambda e: e.memset(nclrow.t[:], 0.0), writes=[nclrow])
        dstrow = nclrow.t[:].rearrange("o (h c) -> o h c", h=H)[:, :, 1:16]
        C.dma("sp", "cli", dstrow,
              cl_dram.t[:, 0:15].rearrange("h c -> () h c"), reads=[cl_db], writes=[nclrow])
        C.op("pe", lambda e: e.matmul(PP[3].t[:, 0:512], ones1f.t[:], nclrow.t[:],
                                      start=True, stop=True),
             reads=[nclrow, ones1f], writes=[PP[3]])
        C.op("dve", lambda e: e.tensor_copy(nclprevT.t[:], PP[3].t[:, 0:512]),
             reads=[PP[3]], writes=[nclprevT])

    with C.phase(pfx + "A2"):
        PT = [C.psum(pfx + f"ptA{i}", [128, 1024], BF16) for i in range(4)]
        PF = [C.psum(pfx + f"pfA{i}", [128, 512], F32) for i in range(4)]
        xr = [C.sbuf(pfx + f"xr{r}", [128, 17 * (2 * CS + 3)], BF16) for r in range(2)]
        ptslot = [[Buf(pfx + f"pts{h}_{s}") for s in range(4)] for h in range(2)]
        pfslot = [Buf(pfx + f"pfs{s}") for s in range(4)]
        cacc = [C.sbuf(pfx + f"cacc{i}", [128, 2 * CS], F32) for i in range(2)]
        bctmp = C.sbuf(pfx + "bctmp", [128, 2 * CS], BF16)
        csil = [C.sbuf(pfx + f"csil{r}", [128, 16 * 2 * CS], BF16) for r in range(2)]
        cpool = C.sbuf(pfx + "cpool", [128, CS], F32)
        xtns = [C.sbuf(pfx + f"xtns{h}", [128, cfg.DI], BF16) for h in range(2)]
        btn = C.sbuf(pfx + "btn", [128, 128], BF16)
        xwh = [C.sbuf(pfx + f"xwh{i}", [128, 64], BF16) for i in range(8)]
        stslot = [Buf(pfx + f"stsl{j}") for j in range(4)]
        w_s = C.sbuf(pfx + "w_s", [32, CS], F32)
        dec_s = C.sbuf(pfx + "dec_s", [32, CS], F32)
        dcwT = C.sbuf(pfx + "dcwT", [128, 128], F32)
        sping = [C.sbuf(pfx + f"sping{r}", [64, H * 64], F32) for r in range(2)]
        nlc = C.sbuf(pfx + "nlc", [32, 2], F32)
        C.op("dve", lambda e: e.memset(sping[1].t[:], 0.0), writes=[sping[1]])

        W2C = 2 * CS + 3

        def load_xbc(cp, r):
            for half in range(2):
                ms, me = (0, 9) if half == 0 else (9, 17)
                src = xbc_d[ms:me, :, cp * 2 * CS:cp * 2 * CS + W2C].rearrange(
                    "m p t -> p m t")
                dst = xr[r].t[:, ms * W2C:me * W2C].rearrange(
                    "p (m t) -> p m t", m=me - ms)
                C.dma("pool", f"xbl{r}{half}", dst, src, reads=[xbc_db], writes=[xr[r]])

        load_xbc(0, 0)
        for c in range(NCH):
            r = (c // 2) % 2
            tc = c * CS
            if c % 2 == 0 and c + 2 < NCH:
                load_xbc(c // 2 + 1, 1 - r)
            C.op("dve", lambda e, tc=tc: e.tensor_scalar_mul(
                nlc.t[:, 0:1], nc_s.t[:, tc + CS - 1:tc + CS], -1.0),
                reads=[nc_s], writes=[nlc])
            C.op("act", lambda e, tc=tc: e.activation(
                out=w_s.t[:], in_=nc_s.t[:, tc:tc + CS], func=AF.Exp,
                scale=1.0, bias=nlc.t[:, 0:1]),
                reads=[nc_s, nlc], writes=[w_s])
            C.op("dve", lambda e, tc=tc: e.tensor_mul(
                w_s.t[:], w_s.t[:], dt_s.t[:, tc:tc + CS]),
                reads=[w_s, dt_s], writes=[w_s])
            if c == 0:
                C.op("act", lambda e: e.activation(out=nlc.t[:, 1:2], in_=nlc.t[:, 0:1],
                                                   func=AF.Exp, scale=1.0),
                     reads=[nlc], writes=[nlc])
            else:
                C.op("dve", lambda e, tc=tc: e.tensor_add(
                    nlc.t[:, 1:2], nc_s.t[:, tc - 1:tc], nlc.t[:, 0:1]),
                    reads=[nc_s, nlc], writes=[nlc])
                C.op("act", lambda e: e.activation(out=nlc.t[:, 1:2], in_=nlc.t[:, 1:2],
                                                   func=AF.Exp),
                     reads=[nlc], writes=[nlc])
            C.op("dve", lambda e: e.tensor_scalar(
                out=dec_s.t[:], in0=w_s.t[:], scalar1=0.0, scalar2=nlc.t[:, 1:2],
                op0=ALU.mult, op1=ALU.add),
                reads=[w_s, nlc], writes=[dec_s])
            if c % 2 == 0:
                for m in range(17):
                    if m == 0:
                        C.act_table(18)
                    base = m * W2C
                    cc = cacc[m % 2]
                    C.op("dve", lambda e, base=base, m=m, r=r, cc=cc: e.tensor_scalar_mul(
                        cc.t[:], xr[r].t[:, base:base + 2 * CS], convw.t[:, m * 4:m * 4 + 1]),
                        reads=[xr[r], convw], writes=[cc])
                    for w in range(1, 4):
                        C.op("dve", lambda e, base=base, m=m, r=r, w=w, cc=cc:
                             e.scalar_tensor_tensor(
                                out=cc.t[:], in0=xr[r].t[:, base + w:base + w + 2 * CS],
                                scalar=convw.t[:, m * 4 + w:m * 4 + w + 1], in1=cc.t[:],
                                op0=ALU.mult, op1=ALU.add),
                             reads=[xr[r], convw, cc], writes=[cc])
                    if m < 16:
                        C.op("act", lambda e, m=m, r=r, cc=cc: e.activation(
                            out=csil[r].t[:, m * 2 * CS:(m + 1) * 2 * CS], in_=cc.t[:],
                            func=AF.Silu, bias=convb.t[:, m:m + 1]),
                            reads=[cc, convb], writes=[csil[r]])
                    else:
                        C.op("act", lambda e, tc=tc, cc=cc: e.activation(
                            out=bctmp.t[:], in_=cc.t[:], func=AF.Silu,
                            bias=convb.t[:, 16:17]),
                            reads=[cc, convb], writes=[bctmp])
                        C.op("act", lambda e, tc=tc: e.copy(
                            out=b_ct.t[:, tc:tc + 2 * CS], in_=bctmp.t[0:64, :]),
                            reads=[bctmp], writes=[b_ct])
                        C.op("pool", lambda e, tc=tc: e.tensor_copy(
                            c0_ct.t[:, tc:tc + 2 * CS], bctmp.t[64:128, :]),
                            reads=[bctmp], writes=[c0_ct])

            C.act_table(6)
            xsrc = csil[r].t[:].rearrange("p (m t2) -> p m t2", m=16)[
                :, :, (c % 2) * CS:(c % 2 + 1) * CS]
            C.dma("act", f"xco{(c // 2) % 2}{c % 2}",
                  xct_d[c, :, :].rearrange("p (m t) -> p m t", m=16),
                  xsrc, reads=[csil[r]], writes=[xct_db[c]])
            for half in range(2):
                for m in range(16):
                    pt = PT[half * 2 + m % 2]
                    pb = ptslot[half][m % 2]
                    off = m * 2 * CS + (c % 2) * CS + half * 128
                    C.op("pe", lambda e, off=off, r=r, pt=pt: e.transpose(
                        pt.t[:, 0:128],
                        csil[r].t[:, off:off + 128],
                        ident.t[:]),
                        reads=[csil[r]], writes=[pb])
                    C.op("act", lambda e, m=m, half=half, pt=pt: e.copy(
                        out=xtns[half].t[:, m * 128:(m + 1) * 128], in_=pt.t[:, 0:128]),
                        reads=[pb], writes=[xtns[half]])
                C.dma("sp", f"xto{half}", xtn_d[c, half, :, :], xtns[half].t[:],
                      reads=[xtns[half]], writes=[xtn_db[c]])
            for half in range(2):
                for bi, band in ((0, w_s), (1, dec_s)):
                    sl = half * 2 + bi
                    pf = PF[sl % 2]
                    pb = pfslot[sl % 2]
                    C.op("pe", lambda e, half=half, band=band, pf=pf: e.transpose(
                        pf.t[0:128, 0:32], band.t[:, half * 128:half * 128 + 128],
                        identf.t[0:32, 0:32]),
                        reads=[band, identf], writes=[pb])
                    C.op("dve", lambda e, half=half, bi=bi, pf=pf: e.tensor_copy(
                        dcwT.t[:, half * 64 + bi * 32:half * 64 + bi * 32 + 32],
                        pf.t[:, 0:32]),
                        reads=[pb], writes=[dcwT])
            for half in range(2):
                pt = PT[half * 2]
                pb = ptslot[half][0]
                C.op("pe", lambda e, tc=tc, half=half, pt=pt: e.transpose(
                    pt.t[:, 0:64],
                    b_ct.t[:, tc + half * 128:tc + half * 128 + 128],
                    ident.t[0:64, 0:64]),
                    reads=[b_ct, ident], writes=[pb])
                C.op("act", lambda e, half=half, pt=pt: e.copy(
                    out=btn.t[:, half * 64:half * 64 + 64], in_=pt.t[:, 0:64]),
                    reads=[pb], writes=[btn])
            def emit_xw(h):
                for half in range(2):
                    xw2 = xwh[(h % 4) * 2 + half]
                    C.op("dve", lambda e, h=h, half=half, xw2=xw2: e.tensor_scalar_mul(
                        xw2.t[:], xtns[half].t[:, h * 64:(h + 1) * 64],
                        dcwT.t[:, half * 64 + h:half * 64 + h + 1]),
                        reads=[xtns[half], dcwT], writes=[xw2])
            emit_xw(0)
            emit_xw(1)
            for h in range(H):
                j = h % 4
                stb = PF[2 + j // 2]
                sco = (j % 2) * 64
                sl = stslot[j]
                xpair = [xwh[j * 2], xwh[j * 2 + 1]]

                def sgrp(e, stb=stb, sco=sco, xpair=xpair):
                    last = None
                    for half in range(2):
                        last = e.matmul(stb.t[0:64, sco:sco + 64],
                                        btn.t[:, half * 64:half * 64 + 64],
                                        xpair[half].t[:],
                                        start=(half == 0), stop=(half == 1))
                    return last
                C.group("pe", sgrp, reads=[btn] + xpair, writes=[sl])
                if h + 2 < H:
                    emit_xw(h + 2)
                C.op("dve", lambda e, h=h, r=r, stb=stb, sco=sco, sl=sl:
                     e.scalar_tensor_tensor(
                        out=sping[r].t[:, h * 64:(h + 1) * 64],
                        in0=sping[1 - r].t[:, h * 64:(h + 1) * 64],
                        scalar=dcwT.t[0:64, 32 + h:32 + h + 1],
                        in1=stb.t[0:64, sco:sco + 64], op0=ALU.mult, op1=ALU.add),
                        reads=[sping[1 - r], dcwT, sl], writes=[sping[r]])
            C.dma("sp", f"sso{r}", s_d[c, :, :], sping[r].t[:], reads=[sping[r]],
                  writes=[s_db[c]])

        if "cc_in" in io:
            cc_in = io["cc_in"]
            cc_db = io["cc_db"]
            dtot = C.sbuf(pfx + "dtot", [32, 1], F32)
            zrow = C.sbuf(pfx + "zrow", [1, 2048], F32)
            C.op("dve", lambda e: e.memset(zrow.t[:], 0.0), writes=[zrow])
            C.op("act", lambda e: e.activation(out=dtot.t[:], in_=nc_s.t[:, T - 1:T],
                                               func=AF.Exp, scale=-1.0),
                 reads=[nc_s], writes=[dtot])
            C.dma("sp", "ccp", cc_in[0:64, :], sping[(NCH - 1) % 2].t[:],
                  reads=[sping[(NCH - 1) % 2]], writes=[cc_db])
            C.dma("sp", "ccp2", cc_in[64:65, :], zrow.t[:], reads=[zrow],
                  writes=[cc_db])
            C.dma("sp", "ccp3", cc_in[64:65, 0:32].rearrange("o h -> h o"),
                  dtot.t[:], reads=[dtot], writes=[cc_db])

    s_in = sin_provider(C) if sin_provider is not None else None

    with C.phase(pfx + "B"):
        PB = [C.psum(pfx + f"pbB{i}", [128, 512], F32) for i in range(8)]
        onehot = C.sbuf(pfx + "onehot", [32, H * 128], F32)
        mask = [C.sbuf(pfx + f"mask{j}", [128, 256], BF16) for j in range(2)]
        for nm, bb in [("onehot", onehot), ("mask0", mask[0]), ("mask1", mask[1])]:
            C.dma("pool", "cstB", bb.t[:], W[nm][:, :], writes=[bb])
        C.fence_stream("cstB", written=[onehot, mask[0], mask[1]])

        eprefT = None
        if s_in is not None:
            eprefT = C.sbuf(pfx + "eprefT", [64, 512], F32)
            C.op("act", lambda e: e.activation(out=eprefT.t[:], in_=nclprevT.t[0:64, :],
                                               func=AF.Exp, scale=-1.0),
                 reads=[nclprevT], writes=[eprefT])

        out_w = [C.sbuf(pfx + f"wout{k}", [128, cfg.D], BF16) for k in range(16)]
        for k in range(16):
            C.dma("pool", "wold", out_w[k].t[:], W["out_wT"][k * 128:(k + 1) * 128, :],
                  writes=[out_w[k]])
        C.fence_stream("wold", written=out_w)
        for k in range(16):
            C.op("dve", lambda e, k=k: e.tensor_scalar_mul(
                out_w[k].t[:], out_w[k].t[:], rmsw.t[:, k:k + 1]),
                reads=[rmsw, out_w[k]], writes=[out_w[k]])
        szr = [C.sbuf(pfx + f"szr{r}", [128, 16 * CS], BF16) for r in range(2)]
        xcr = [C.sbuf(pfx + f"xcr{r}", [128, 16 * CS], BF16) for r in range(2)]
        xtr = [[C.sbuf(pfx + f"xtr{r}_{half}", [128, cfg.DI], BF16) for half in range(2)]
               for r in range(2)]
        slr = [C.sbuf(pfx + f"slr{r}", [64, H * 64], F32) for r in range(2)]
        C.op("dve", lambda e: e.memset(slr[0].t[:], 0.0), writes=[slr[0]])
        sgjit = [C.sbuf(pfx + f"sgjit{i}", [64, 64], BF16) for i in range(2)]
        dcwTB = C.sbuf(pfx + "dcwTB", [128, 256], F32)
        pbslot = [Buf(pfx + f"pbs{s}") for s in range(4)]
        sm = [[C.sbuf(pfx + f"smB{i}_{j}", [128, 256], BF16) for j in range(2)]
              for i in range(2)]
        et = [C.sbuf(pfx + f"etB{j}", [128, 256], F32) for j in range(2)]
        mt = [[C.sbuf(pfx + f"mtB{i}_{j}", [128, 256], BF16) for j in range(2)] for i in range(4)]
        cet = [C.sbuf(pfx + f"cetB{i}", [64, 256], F32) for i in range(3)]
        cexp = [C.sbuf(pfx + f"cexpB{i}", [64, 256], BF16) for i in range(3)]
        ysq = [C.sbuf(pfx + f"ysq{i}", [128, CS], F32) for i in range(2)]
        ysb = C.sbuf(pfx + "ysb", [128, 16 * 2 * CS], BF16)
        rstdB = C.sbuf(pfx + "rstdB", [1, CS], F32)
        onew = ysq

        def load_chunk(c, r):
            C.dma("sp", f"szl{r}", szr[r].t[:], sz_d[c, :, :], reads=[sz_db],
                  writes=[szr[r]])
            C.dma("sp", f"xcl{r}", xcr[r].t[:], xct_d[c, :, :], reads=[xct_db[c]],
                  writes=[xcr[r]])
            for half in range(2):
                C.dma("sp", f"xtl{r}{half}", xtr[r][half].t[:], xtn_d[c, half, :, :],
                      reads=[xtn_db[c]], writes=[xtr[r][half]])
            if c > 0:
                C.dma("sp", f"sll{r}", slr[r].t[:], s_d[c - 1, :, :],
                      reads=[s_db[c - 1]], writes=[slr[r]])

        load_chunk(0, 0)
        load_chunk(1, 1)
        for cp in range(NCH // 2):
            for c01 in range(2):
                c = 2 * cp + c01
                tc = c * CS
                for half in range(2):
                    for bi, band in ((0, dt_s), (1, nc_s)):
                        C.op("pe", lambda e, tc=tc, half=half, band=band: e.transpose(
                            PB[1].t[0:128, 0:32],
                            band.t[:, tc + half * 128:tc + half * 128 + 128],
                            identf.t[0:32, 0:32]),
                            reads=[band, identf], writes=[pbslot[0]])
                        sgn = 1.0 if bi == 0 else -1.0
                        C.op("dve", lambda e, c01=c01, half=half, bi=bi, sgn=sgn:
                             e.tensor_scalar_mul(
                                dcwTB.t[:, c01 * 128 + half * 64 + bi * 32:
                                        c01 * 128 + half * 64 + bi * 32 + 32],
                                PB[1].t[:, 0:32], sgn),
                             reads=[pbslot[0]], writes=[dcwTB])
                for half in range(2):
                    C.op("pe", lambda e, tc=tc, c01=c01, half=half: e.matmul(
                        PB[0].t[:, c01 * 256 if False else (half * 256):(half + 1) * 256]
                        if False else PB[0].t[:, half * 256:(half + 1) * 256],
                        b_ct.t[:, tc + half * 128:tc + half * 128 + 128],
                        c0_ct.t[:, tc:tc + CS],
                        start=True, stop=True),
                        reads=[b_ct, c0_ct], writes=[PB[0]])
                    C.op("dve", lambda e, c01=c01, half=half: e.tensor_mul(
                        sm[c01][half].t[:], PB[0].t[:, half * 256:(half + 1) * 256],
                        mask[half].t[:]),
                        reads=[PB[0], mask[half]], writes=[sm[c01][half]])
            tcp = 2 * cp * CS

            def emit_bcast(h):
                pbc = PB[1 + h % 3]
                C.op("pe", lambda e, h=h, tcp=tcp, pbc=pbc: e.matmul(
                    pbc.t[:, 0:512], onehot.t[:, h * 128:(h + 1) * 128],
                    nc_s.t[:, tcp:tcp + 512], start=True, stop=True),
                    reads=[onehot, nc_s], writes=[pbc])
            emit_bcast(0)
            for h in range(H):
                hr2 = h % 2
                pbc = PB[1 + h % 3]
                if h + 1 < H:
                    emit_bcast(h + 1)
                for c01 in range(2):
                    c = 2 * cp + c01
                    tc = c * CS
                    r = c % 2
                    hq = hr2 * 2 + c01
                    hq3 = (h * 2 + c01) % 3
                    ybank = PB[4 + c01 * 2 + ((h // 2) % 2)]
                    for half in range(2):
                        eb = et[half]
                        mb = mt[hq][half]
                        C.op("act", lambda e, h=h, c01=c01, half=half, pbc=pbc, eb=eb:
                             e.activation(
                                out=eb.t[:], in_=pbc.t[:, c01 * 256:c01 * 256 + 256],
                                func=AF.Relu, scale=1.0,
                                bias=dcwTB.t[:, c01 * 128 + half * 64 + 32 + h:
                                             c01 * 128 + half * 64 + 32 + h + 1]),
                             reads=[pbc, dcwTB], writes=[eb])
                        C.op("act", lambda e, eb=eb, mb=mb: e.activation(
                            out=mb.t[:], in_=eb.t[:], func=AF.Exp, scale=-1.0),
                            reads=[eb], writes=[mb])
                        C.op("dve", lambda e, h=h, c01=c01, half=half, mb=mb:
                             e.scalar_tensor_tensor(
                                out=mb.t[:], in0=mb.t[:],
                                scalar=dcwTB.t[:, c01 * 128 + half * 64 + h:
                                               c01 * 128 + half * 64 + h + 1],
                                in1=sm[c01][half].t[:], op0=ALU.mult, op1=ALU.mult),
                             reads=[mb, dcwTB, sm[c01][half]], writes=[mb])
                    sgj = sgjit[c01]
                    slprev = slr[c % 2]
                    if s_in is not None:
                        C.op("dve", lambda e, c=c, h=h, slprev=slprev, sgj=sgj:
                             e.scalar_tensor_tensor(
                                out=sgj.t[:], in0=s_in.t[:, h * 64:(h + 1) * 64],
                                scalar=eprefT.t[:, h * 16 + c:h * 16 + c + 1],
                                in1=slprev.t[:, h * 64:(h + 1) * 64],
                                op0=ALU.mult, op1=ALU.add),
                             reads=[s_in, eprefT, slprev], writes=[sgj])
                    else:
                        C.op("dve", lambda e, h=h, slprev=slprev, sgj=sgj: e.tensor_copy(
                            sgj.t[:], slprev.t[:, h * 64:(h + 1) * 64]),
                            reads=[slprev], writes=[sgj])
                    cb = cet[hq3]
                    C.op("act", lambda e, h=h, c=c, c01=c01, pbc=pbc, cb=cb: e.activation(
                        out=cb.t[:], in_=pbc.t[0:64, c01 * 256:c01 * 256 + 256],
                        func=AF.Exp, scale=-1.0,
                        bias=nclprevT.t[0:64, h * 16 + c:h * 16 + c + 1]),
                        reads=[pbc, nclprevT], writes=[cb])
                    C.op("pool", lambda e, tc=tc, hq3=hq3, cb=cb: e.tensor_mul(
                        cexp[hq3].t[:], cb.t[:], c0_ct.t[:, tc:tc + CS]),
                        reads=[cb, c0_ct], writes=[cexp[hq3]])

                    ro = (h % 2) * 64
                    tp = None if ro == 0 else (0, 64)

                    def ygrp(e, h=h, r=r, hq=hq, hq3=hq3, c01=c01, ybank=ybank, ro=ro, tp=tp):
                        kw = {} if tp is None else {"tile_position": tp}
                        e.matmul(ybank.t[ro:ro + 64, 0:256],
                                 xtr[r][0].t[:, h * 64:(h + 1) * 64],
                                 mt[hq][0].t[:], start=True, stop=False, **kw)
                        e.matmul(ybank.t[ro:ro + 64, 0:256],
                                 xtr[r][1].t[:, h * 64:(h + 1) * 64],
                                 mt[hq][1].t[:], start=False, stop=False, **kw)
                        return e.matmul(ybank.t[ro:ro + 64, 0:256],
                                        sgjit[c01].t[:],
                                        cexp[hq3].t[:], start=False, stop=True, **kw)
                    C.group("pe", ygrp, reads=[xtr[r][0], xtr[r][1], mt[hq][0],
                                               mt[hq][1], sgjit[c01], cexp[hq3]],
                            writes=[ybank])
                    if ro == 64:
                        mtile = h // 2
                        yo = mtile * 2 * CS + c01 * CS
                        C.op("dve", lambda e, r=r, yo=yo, mtile=mtile, ybank=ybank:
                             e.scalar_tensor_tensor(
                                out=ysb.t[:, yo:yo + CS],
                                in0=xcr[r].t[:, mtile * CS:(mtile + 1) * CS],
                                scalar=dexp.t[:, mtile:mtile + 1],
                                in1=ybank.t[:, 0:256], op0=ALU.mult, op1=ALU.add),
                             reads=[xcr[r], dexp, ybank], writes=[ysb])
            for c01 in range(2):
                c = 2 * cp + c01
                r = c % 2
                for m in range(16):
                    yo = m * 2 * CS + c01 * CS
                    C.op("dve", lambda e, m=m, r=r, yo=yo: e.tensor_mul(
                        ysb.t[:, yo:yo + CS], ysb.t[:, yo:yo + CS],
                        szr[r].t[:, m * CS:(m + 1) * CS]),
                        reads=[ysb, szr[r]], writes=[ysb])
                for m in range(16):
                    yq = ysq[m % 2]
                    yo = m * 2 * CS + c01 * CS
                    C.op("act", lambda e, yq=yq, yo=yo: e.activation(
                        out=yq.t[:], in_=ysb.t[:, yo:yo + CS], func=AF.Square),
                        reads=[ysb], writes=[yq])
                    C.op("pe", lambda e, m=m, yq=yq: e.matmul(
                        PB[0].t[0:1, 0:256], onescol.t[:], yq.t[:],
                        start=(m == 0), stop=(m == 15)),
                        reads=[yq, onescol], writes=[PB[0]])
                C.op("act", lambda e: e.activation(out=rstdB.t[:], in_=PB[0].t[0:1, 0:256],
                                                   func=AF.Ln, scale=1.0 / cfg.DI,
                                                   bias=cstc.t[0:1, 0:1]),
                     reads=[PB[0], cstc], writes=[rstdB])
                C.op("act", lambda e: e.activation(out=rstdB.t[:], in_=rstdB.t[:],
                                                   func=AF.Exp, scale=-0.5),
                     reads=[rstdB], writes=[rstdB])
                C.op("pe", lambda e: e.matmul(PB[0].t[:, 0:256], ones1f.t[:], rstdB.t[:],
                                              start=True, stop=True),
                     reads=[rstdB, ones1f], writes=[PB[0]])
                for m in range(16):
                    yo = m * 2 * CS + c01 * CS
                    C.op("dve", lambda e, yo=yo: e.tensor_mul(
                        ysb.t[:, yo:yo + CS], ysb.t[:, yo:yo + CS],
                        PB[0].t[:, 0:256]),
                        reads=[PB[0], ysb], writes=[ysb])
            tc0 = 2 * cp * CS
            for m in range(8):
                ob = PB[4 + m % 4]

                def ogrp(e, m=m, ob=ob):
                    last = None
                    for k in range(16):
                        last = e.matmul(ob.t[:, 0:512],
                                        out_w[k].t[:, m * 128:(m + 1) * 128],
                                        ysb.t[:, k * 2 * CS:(k + 1) * 2 * CS],
                                        start=(k == 0), stop=(k == 15))
                    return last
                C.group("pe", ogrp, reads=out_w + [ysb], writes=[ob])
                for c01 in range(2):
                    onw = onew[c01]
                    C.op("dve", lambda e, ob=ob, onw=onw, c01=c01: e.tensor_copy(
                        onw.t[:], ob.t[:, c01 * 256:c01 * 256 + 256]),
                        reads=[ob], writes=[onw])
                    C.dma("pool", f"hacc{c01}",
                          h_out[m * 128:(m + 1) * 128, tc0 + c01 * CS:tc0 + c01 * CS + CS],
                          onw.t[:], reads=[onw],
                          writes=[hout_db[2 * cp + c01][m]], accum_op=ALU.add)
            if 2 * cp + 2 < NCH:
                load_chunk(2 * cp + 2, 0)
            if 2 * cp + 3 < NCH:
                load_chunk(2 * cp + 3, 1)


def mamba_host_weights(in_w, conv_w, conv_b, dt_bias, A_log, Dp, rms_w, out_w, norm_w):
    import numpy as np
    import ml_dtypes
    f32 = np.float32
    out = {}
    out["w_inT"] = np.ascontiguousarray(in_w.T.astype(f32))
    out["out_wT"] = np.ascontiguousarray(out_w.T.astype(f32))
    out["conv_w"] = np.ascontiguousarray(
        conv_w.reshape(17, 128, 4).transpose(1, 0, 2).reshape(128, 68).astype(f32))
    out["conv_b"] = np.ascontiguousarray(conv_b.reshape(17, 128).T.astype(f32))
    out["dt_bias"] = dt_bias.reshape(32, 1).astype(f32)
    out["A_log"] = A_log.reshape(32, 1).astype(f32)
    out["Dexp"] = np.ascontiguousarray(
        np.repeat(Dp, 64).reshape(16, 128).T.astype(f32))
    out["rms_w"] = np.ascontiguousarray(rms_w.reshape(16, 128).T.astype(f32))
    out["norm_w"] = np.ascontiguousarray(norm_w.reshape(8, 128).T.astype(f32))
    oh = np.zeros((32, 32 * 128), f32)
    for h in range(32):
        oh[h, h * 128:(h + 1) * 128] = 1.0
    out["onehot"] = oh
    j = np.arange(128)[:, None]
    i = np.arange(256)[None, :]
    out["mask0"] = (i >= j).astype(ml_dtypes.bfloat16)
    out["mask1"] = (i >= j + 128).astype(ml_dtypes.bfloat16)
    out["ident"] = np.eye(128, dtype=ml_dtypes.bfloat16)
    out["identf"] = np.eye(128, dtype=f32)
    return out


def mamba_declare_weights(nc, pfx):
    import concourse.mybir as mybir
    F = mybir.dt.float32
    B = mybir.dt.bfloat16
    d = {}
    spec = {
        "w_inT": ([1024, 4256], F), "out_wT": ([2048, 1024], F),
        "conv_w": ([128, 68], F), "conv_b": ([128, 17], F),
        "dt_bias": ([32, 1], F), "A_log": ([32, 1], F),
        "Dexp": ([128, 16], F), "rms_w": ([128, 16], F), "norm_w": ([128, 8], F),
        "onehot": ([32, 32 * 128], F),
        "mask0": ([128, 256], B), "mask1": ([128, 256], B),
        "ident": ([128, 128], B), "identf": ([128, 128], F),
    }
    for nm, (shape, dt) in spec.items():
        d[nm] = nc.dram_tensor(pfx + nm, shape, dt, kind="ExternalInput").ap()
    return d


def mamba_declare_scratch(nc, cfg, pfx):
    import concourse.mybir as mybir
    F = mybir.dt.float32
    B = mybir.dt.bfloat16
    return {
        "sz_d": nc.dram_tensor(pfx + "sz_d", [cfg.NCH, 128, 16 * cfg.CS], B).ap(),
        "xbc_d": nc.dram_tensor(pfx + "xbc_d", [17, 128, cfg.T + 3], B).ap(),
        "xct_d": nc.dram_tensor(pfx + "xct_d", [cfg.NCH, 128, 16 * cfg.CS], B).ap(),
        "xtn_d": nc.dram_tensor(pfx + "xtn_d", [cfg.NCH, 2, 128, cfg.DI], B).ap(),
        "s_d": nc.dram_tensor(pfx + "s_d", [cfg.NCH, 64, cfg.H * 64], F).ap(),
    }


"""Top-level kernel assembly parts: LN pass, routing, FFN, collectives, kernel()."""



I32 = mybir.dt.int32


def emit_ln_pass(C, cfg, pfx, u_ap, h2_ap, x_out, ln_out, halo_in, halo_out, add_res, ln_f32=False):
    T, D, KD = cfg.T, cfg.D, cfg.KD
    with C.phase(pfx + "LN"):
        PP = [C.psum(pfx + f"lnp{i}", [128, 512], F32) for i in range(2)]
        ub = [[C.sbuf(pfx + f"lnu{r}_{k}", [128, 512], F32) for k in range(KD)] for r in range(2)]
        hb = [[C.sbuf(pfx + f"lnh{r}_{k}", [128, 512], F32) for k in range(KD)] for r in range(2)]
        xb = [[C.sbuf(pfx + f"lnx{r}_{k}", [128, 512], F32) for k in range(KD)] for r in range(2)]
        sq = [C.sbuf(pfx + f"lnsq{k}", [128, 512], F32) for k in range(KD)]
        lb = [C.sbuf(pfx + f"lnl{k}", [128, 512], F32 if ln_f32 else BF16) for k in range(KD)]
        mrow = C.sbuf(pfx + "lnm", [1, 512], F32)
        rrow = C.sbuf(pfx + "lnr", [1, 512], F32)
        onescol = C.sbuf(pfx + "lnoc", [128, 1], F32)
        ones1f = C.sbuf(pfx + "lno1", [1, 128], F32)
        cstc = C.sbuf(pfx + "lncst", [128, 1], F32)
        C.op("dve", lambda e: e.memset(onescol.t[:], 1.0 / D), writes=[onescol])
        C.op("dve", lambda e: e.memset(ones1f.t[:], 1.0), writes=[ones1f])
        C.op("dve", lambda e: e.memset(cstc.t[:], cfg.EPS), writes=[cstc])
        C.act_table(6)

        ntg = T // 512
        for g in range(ntg + (1 if halo_in is not None else 0)):
            halo_g = (g == ntg)
            r = g % 2
            t0 = g * 512
            N = 3 if halo_g else 512
            for k in range(KD):
                src = halo_in if halo_g else u_ap
                c0, c1 = (0, 3) if halo_g else (t0, t0 + 512)
                C.dma("sp", f"lnul{r}", ub[r][k].t[:, 0:N],
                      src[k * 128:(k + 1) * 128, c0:c1], writes=[ub[r][k]])
            C.fence_stream(f"lnul{r}", written=ub[r])
            if add_res and not halo_g:
                for k in range(KD):
                    C.dma("sp", f"lnhl{r}", hb[r][k].t[:, 0:N],
                          h2_ap[k * 128:(k + 1) * 128, t0:t0 + 512], writes=[hb[r][k]])
                C.fence_stream(f"lnhl{r}", written=hb[r])
                for k in range(KD):
                    C.op("dve", lambda e, k=k, r=r: e.tensor_add(
                        xb[r][k].t[:], ub[r][k].t[:], hb[r][k].t[:]),
                        reads=[ub[r][k], hb[r][k]], writes=[xb[r][k]])
                xcur = xb[r]
            else:
                xcur = ub[r]
            for k in range(KD):
                C.op("pe", lambda e, k=k, xc=xcur, N=N: e.matmul(
                    PP[0].t[0:1, 0:N], onescol.t[:], xc[k].t[:, 0:N],
                    start=(k == 0), stop=(k == KD - 1)),
                    reads=[xcur[k], onescol], writes=[PP[0]])
            C.op("dve", lambda e, N=N: e.tensor_copy(mrow.t[:, 0:N], PP[0].t[0:1, 0:N]),
                 reads=[PP[0]], writes=[mrow])
            for k in range(KD):
                C.op("act", lambda e, k=k, xc=xcur, N=N: e.activation(
                    out=sq[k].t[:, 0:N], in_=xc[k].t[:, 0:N], func=AF.Square),
                    reads=[xcur[k]], writes=[sq[k]])
                C.op("pe", lambda e, k=k, N=N: e.matmul(
                    PP[0].t[0:1, 0:N], onescol.t[:], sq[k].t[:, 0:N],
                    start=(k == 0), stop=(k == KD - 1)),
                    reads=[sq[k], onescol], writes=[PP[0]])
            C.op("dve", lambda e, N=N: e.scalar_tensor_tensor(
                out=rrow.t[:, 0:N], in0=mrow.t[:, 0:N], scalar=-1.0, in1=mrow.t[:, 0:N],
                op0=ALU.mult, op1=ALU.mult),
                reads=[mrow], writes=[rrow])
            C.op("dve", lambda e, N=N: e.tensor_add(
                rrow.t[:, 0:N], rrow.t[:, 0:N], PP[0].t[0:1, 0:N]),
                reads=[rrow, PP[0]], writes=[rrow])
            C.op("act", lambda e, N=N: e.activation(
                out=rrow.t[:, 0:N], in_=rrow.t[:, 0:N], func=AF.Ln, bias=cstc.t[0:1, :]),
                reads=[rrow, cstc], writes=[rrow])
            C.op("act", lambda e, N=N: e.activation(
                out=rrow.t[:, 0:N], in_=rrow.t[:, 0:N], func=AF.Exp, scale=-0.5),
                reads=[rrow], writes=[rrow])
            C.op("pe", lambda e, N=N: e.matmul(PP[0].t[:, 0:N], ones1f.t[:], mrow.t[:, 0:N],
                                               start=True, stop=True),
                 reads=[mrow, ones1f], writes=[PP[0]])
            C.op("pe", lambda e, N=N: e.matmul(PP[1].t[:, 0:N], ones1f.t[:], rrow.t[:, 0:N],
                                               start=True, stop=True),
                 reads=[rrow, ones1f], writes=[PP[1]])
            for k in range(KD):
                C.op("dve", lambda e, k=k, xc=xcur, N=N: e.tensor_sub(
                    sq[k].t[:, 0:N], xc[k].t[:, 0:N], PP[0].t[:, 0:N]),
                    reads=[xcur[k], PP[0]], writes=[sq[k]])
                C.op("dve", lambda e, k=k, N=N: e.tensor_mul(
                    lb[k].t[:, 0:N], sq[k].t[:, 0:N], PP[1].t[:, 0:N]),
                    reads=[sq[k], PP[1]], writes=[lb[k]])
            for k in range(KD):
                if halo_g:
                    C.op("dve", lambda e, k=k: e.tensor_mul(
                        sq[k].t[:, 0:3], sq[k].t[:, 0:3], PP[1].t[:, 0:3]),
                        reads=[sq[k], PP[1]], writes=[sq[k]])
                    C.dma("sp", "lnho", halo_out[k * 128:(k + 1) * 128, :],
                          sq[k].t[:, 0:3], reads=[sq[k]])
                else:
                    C.dma("sp", f"lnlo{r}", ln_out[k * 128:(k + 1) * 128, t0:t0 + 512],
                          lb[k].t[:, 0:N], reads=[lb[k]])
                    if x_out is not None:
                        C.dma("sp", f"lnxo{r}", x_out[k * 128:(k + 1) * 128, t0:t0 + 512],
                              xcur[k].t[:, 0:N], reads=[xcur[k]])


def emit_routing(C, cfg, pfx, W, ln_d, rout_d):
    T, D, KD = cfg.T, cfg.D, cfg.KD
    NW = 4
    with C.phase(pfx + "RT"):
        PP = [C.psum(pfx + f"rtp{i}", [128, 512], F32) for i in range(NW)]
        gw = C.sbuf(pfx + "rtgw", [128, 8 * KD], BF16)
        gb = C.sbuf(pfx + "rtgb", [8, 1], F32)
        lnb = [C.sbuf(pfx + f"rtln{k}", [128, 512], BF16) for k in range(KD)]
        lg = C.sbuf(pfx + "rtlg", [8, T], F32)
        lgo = C.sbuf(pfx + "rtlgo", [8, T], F32)
        lgT = [C.sbuf(pfx + f"rtlgT{i}", [128, 8], F32) for i in range(NW)]
        m1 = [C.sbuf(pfx + f"rtm1{i}", [128, 1], F32) for i in range(NW)]
        m2 = [C.sbuf(pfx + f"rtm2{i}", [128, 1], F32) for i in range(NW)]
        w1 = [C.sbuf(pfx + f"rtw1{i}", [128, 1], F32) for i in range(NW)]
        w2 = [C.sbuf(pfx + f"rtw2{i}", [128, 1], F32) for i in range(NW)]
        eq = [C.sbuf(pfx + f"rteq{i}", [128, 8], F32) for i in range(NW)]
        cumt = [C.sbuf(pfx + f"rtcum{i}", [128, 8], F32) for i in range(NW)]
        fst = [[C.sbuf(pfx + f"rtf{i}_{j}", [128, 8], F32) for j in range(2)]
               for i in range(NW)]
        l2 = [C.sbuf(pfx + f"rtl2{i}", [128, 8], F32) for i in range(NW)]
        rt = [C.sbuf(pfx + f"rtr{i}", [128, 8], F32) for i in range(NW)]
        zer8 = C.sbuf(pfx + "rtz", [128, 8], F32)
        identf = C.sbuf(pfx + "rtid", [128, 128], F32)
        C.dma("pool", "rtc2", gw.t[:], W["gate_wT"][:, :], writes=[gw])
        C.dma("sp", "rtc", gb.t[:], W["gate_b"][:, :], writes=[gb])
        C.dma("sp", "rtc", identf.t[:], W["identf2"][:, :], writes=[identf])
        C.fence_stream("rtc", written=[gb, identf])
        C.fence_stream("rtc2", written=[gw])
        C.op("dve", lambda e: e.memset(zer8.t[:], 0.0), writes=[zer8])
        C.act_table(2)

        for g in range(T // 512):
            t0 = g * 512
            r = g % 2
            for k in range(KD):
                C.dma("sp", f"rtll{r}", lnb[k].t[:], ln_d[k * 128:(k + 1) * 128, t0:t0 + 512],
                      writes=[lnb[k]])
            C.fence_stream(f"rtll{r}", written=lnb)

            def ggrp(e, t0=t0):
                last = None
                for k in range(KD):
                    last = e.matmul(PP[0].t[0:8, 0:512],
                                    gw.t[:, k * 8:(k + 1) * 8], lnb[k].t[:],
                                    start=(k == 0), stop=(k == KD - 1))
                return last
            C.group("pe", ggrp, reads=[gw] + lnb, writes=[PP[0]])
            C.op("dve", lambda e, t0=t0: e.tensor_scalar_add(
                lg.t[:, t0:t0 + 512], PP[0].t[0:8, 0:512], gb.t[:]),
                reads=[PP[0], gb], writes=[lg])
        def tiles_of(i0):
            return [(i0 + j, j) for j in range(NW) if i0 + j < T // 128]

        for i0 in range(0, T // 128, NW):
            tl = tiles_of(i0)
            for i, r in tl:
                C.op("pe", lambda e, i=i, pb=PP[r]: e.transpose(
                    pb.t[0:128, 0:8], lg.t[:, i * 128:(i + 1) * 128], identf.t[0:8, 0:8]),
                    reads=[lg, identf], writes=[PP[r]])
            for i, r in tl:
                C.op("dve", lambda e, lt=lgT[r], pb=PP[r]: e.tensor_copy(lt.t[:], pb.t[:, 0:8]),
                     reads=[PP[r]], writes=[lgT[r]])
            for i, r in tl:
                C.op("dve", lambda e, lt=lgT[r], m_1=m1[r]: e.reduce_max(
                    m_1.t[:], lt.t[:], axis=AX.X), reads=[lgT[r]], writes=[m1[r]])
            for i, r in tl:
                C.op("dve", lambda e, lt=lgT[r], e_q=eq[r], m_1=m1[r]: e.tensor_single_scalar(
                    e_q.t[:], lt.t[:], m_1.t[:], ALU.is_equal),
                    reads=[lgT[r], m1[r]], writes=[eq[r]])
            for i, r in tl:
                C.op("dve", lambda e, e_q=eq[r], cu=cumt[r]: e.tensor_tensor_scan(
                    out=cu.t[:], data0=e_q.t[:], data1=zer8.t[:], initial=0.0,
                    op0=ALU.add, op1=ALU.add), reads=[eq[r], zer8], writes=[cumt[r]])
            for i, r in tl:
                C.op("dve", lambda e, cu=cumt[r]: e.tensor_single_scalar(
                    cu.t[:], cu.t[:], 1.0, ALU.is_equal), reads=[cumt[r]], writes=[cumt[r]])
            for i, r in tl:
                C.op("dve", lambda e, e_q=eq[r], cu=cumt[r], f_s=fst[r]: e.tensor_mul(
                    f_s[0].t[:], e_q.t[:], cu.t[:]), reads=[eq[r], cumt[r]], writes=[fst[r][0]])
            for i, r in tl:
                C.op("dve", lambda e, lt=lgT[r], f_s=fst[r], l_2=l2[r]: e.scalar_tensor_tensor(
                    out=l_2.t[:], in0=f_s[0].t[:], scalar=-1e30, in1=lt.t[:],
                    op0=ALU.mult, op1=ALU.add), reads=[fst[r][0], lgT[r]], writes=[l2[r]])
            for i, r in tl:
                C.op("dve", lambda e, l_2=l2[r], m_2=m2[r]: e.reduce_max(
                    m_2.t[:], l_2.t[:], axis=AX.X), reads=[l2[r]], writes=[m2[r]])
            for i, r in tl:
                C.op("dve", lambda e, l_2=l2[r], m_2=m2[r], e_q=eq[r]: e.tensor_single_scalar(
                    e_q.t[:], l_2.t[:], m_2.t[:], ALU.is_equal),
                    reads=[l2[r], m2[r]], writes=[eq[r]])
            for i, r in tl:
                C.op("dve", lambda e, e_q=eq[r], cu=cumt[r]: e.tensor_tensor_scan(
                    out=cu.t[:], data0=e_q.t[:], data1=zer8.t[:], initial=0.0,
                    op0=ALU.add, op1=ALU.add), reads=[eq[r], zer8], writes=[cumt[r]])
            for i, r in tl:
                C.op("dve", lambda e, cu=cumt[r]: e.tensor_single_scalar(
                    cu.t[:], cu.t[:], 1.0, ALU.is_equal), reads=[cumt[r]], writes=[cumt[r]])
            for i, r in tl:
                C.op("dve", lambda e, e_q=eq[r], cu=cumt[r], f_s=fst[r]: e.tensor_mul(
                    f_s[1].t[:], e_q.t[:], cu.t[:]), reads=[eq[r], cumt[r]], writes=[fst[r][1]])
            for i, r in tl:
                C.op("dve", lambda e, m_1=m1[r], m_2=m2[r], w_1=w1[r]: e.tensor_sub(
                    w_1.t[:], m_1.t[:], m_2.t[:]), reads=[m1[r], m2[r]], writes=[w1[r]])
            for i, r in tl:
                C.op("act", lambda e, w_1=w1[r], w_2=w2[r]: e.activation(
                    out=w_2.t[:], in_=w_1.t[:], func=AF.Sigmoid, scale=-1.0),
                    reads=[w1[r]], writes=[w2[r]])
            for i, r in tl:
                C.op("act", lambda e, w_1=w1[r]: e.activation(
                    out=w_1.t[:], in_=w_1.t[:], func=AF.Sigmoid),
                    reads=[w1[r]], writes=[w1[r]])
            for i, r in tl:
                C.op("dve", lambda e, ro=rt[r], f_s=fst[r], w_1=w1[r]: e.tensor_scalar_mul(
                    ro.t[:], f_s[0].t[:], w_1.t[:]), reads=[fst[r][0], w1[r]], writes=[rt[r]])
            for i, r in tl:
                C.op("dve", lambda e, ro=rt[r], f_s=fst[r], w_2=w2[r]: e.scalar_tensor_tensor(
                    out=ro.t[:], in0=f_s[1].t[:], scalar=w_2.t[:], in1=ro.t[:],
                    op0=ALU.mult, op1=ALU.add), reads=[fst[r][1], w2[r], rt[r]], writes=[rt[r]])
            for i, r in tl:
                C.op("pe", lambda e, i=i, ro=rt[r], pb=PP[r]: e.transpose(
                    pb.t[0:8, 0:128], ro.t[:], identf.t[:]),
                    reads=[rt[r], identf], writes=[PP[r]])
            for i, r in tl:
                C.op("dve", lambda e, i=i, pb=PP[r]: e.tensor_copy(
                    lgo.t[:, i * 128:(i + 1) * 128], pb.t[0:8, 0:128]),
                    reads=[PP[r]], writes=[lgo])
        for q in range(2):
            C.dma("sp", "rto", rout_d[:, q * (T // 2):(q + 1) * (T // 2)],
                  lgo.t[0:8, q * (T // 2):(q + 1) * (T // 2)], reads=[lgo])
        C.fence_stream("rto", written=[])


def emit_ffn_prep(C, cfg, pfx, W):
    MF = 32
    w1sb = C.sbuf(pfx + "w1sb", [128, MF * 1024], BF16)
    w2sb = C.sbuf(pfx + "w2sb", [128, MF * 1024], BF16)
    b1sb = C.sbuf(pfx + "ffb1", [128, 8 * MF], F32)
    b2sb = C.sbuf(pfx + "ffb2c", [128, 64], F32)
    C.dma("sp", "ffc", b1sb.t[:], W["b1h"][:, :], writes=[b1sb])
    C.dma("sp", "ffc", b2sb.t[:], W["b2c"][:, :], writes=[b2sb])
    C.fence_stream("ffc", written=[b1sb, b2sb])

    def load_w1(e_i):
        for g in range(8):
            mf = 4 * g
            src = W["w1t"][e_i * MF + mf:e_i * MF + mf + 4, :, :].rearrange(
                "m p c -> p m c")
            dst = w1sb.t[:, mf * 1024:(mf + 4) * 1024].rearrange(
                "p (m c) -> p m c", m=4)
            C.dma("pool", "fw1", dst, src, writes=[w1sb])
        C.fence_stream("fw1", written=[w1sb])

    def load_w2(e_i):
        for mfq in range(8):
            src = W["w2"][e_i].rearrange("(q m p) c -> q m p c", q=8, p=128)[mfq]
            src = src.rearrange("m p c -> p m c")
            dst = w2sb.t[:, mfq * 4096:(mfq + 1) * 4096].rearrange(
                "p (m c) -> p m c", m=4)
            C.dma("pool", "fw2", dst, src, writes=[w2sb])
        C.fence_stream("fw2", written=[w2sb])

    load_w1(0)
    load_w2(0)
    return {"w1sb": w1sb, "w2sb": w2sb, "b1sb": b1sb, "b2sb": b2sb,
            "load_w1": load_w1, "load_w2": load_w2}


def emit_ffn(C, cfg, pfx, W, ln_d, rout_d, out_ext, out_db, st):
    T, D = cfg.T, cfg.D
    MF = 32
    NT = T // 512
    w1sb, w2sb = st["w1sb"], st["w2sb"]
    b1sb, b2sb = st["b1sb"], st["b2sb"]
    with C.phase(pfx + "FF"):
        PS = [C.psum(pfx + f"ffp{i}", [128, 512], F32) for i in range(8)]
        h_sb = C.sbuf(pfx + "ffh", [128, MF * 512], BF16)
        h_t = [Buf(pfx + f"ffh{mf}", h_sb.t) for mf in range(MF)]
        lnr = [C.sbuf(pfx + f"fflnr{r}", [128, 8 * 512], BF16) for r in range(2)]
        wb = [C.sbuf(pfx + f"ffwbr{r}", [128, 512], F32) for r in range(2)]
        wer = [C.sbuf(pfx + f"ffwe{r}", [1, 512], F32) for r in range(2)]
        tmp = [C.sbuf(pfx + f"fftm{i}", [128, 512], F32) for i in range(4)]
        ones1f = C.sbuf(pfx + "ffo1", [1, 128], F32)
        C.op("dve", lambda e: e.memset(ones1f.t[:], 1.0), writes=[ones1f])
        C.act_table(10)

        def load_ln(e_i, tb, r):
            t0 = tb * 512
            src = ln_d[:, t0:t0 + 512].rearrange("(k p) t -> p k t", k=8)
            dst = lnr[r].t[:].rearrange("p (k t) -> p k t", k=8)
            C.dma("sp", f"flnl{r}", dst, src, writes=[lnr[r]])
            C.dma("sp", f"fwer{r}", wer[r].t[:], rout_d[e_i:e_i + 1, t0:t0 + 512],
                  writes=[wer[r]])

        steps = [(e_i, tb) for e_i in range(8) for tb in range(NT)]
        load_ln(0, 0, 0)
        for si, (e_i, tb) in enumerate(steps):
            r = si % 2
            t0 = tb * 512
            if si + 1 < len(steps):
                load_ln(steps[si + 1][0], steps[si + 1][1], 1 - r)
            C.op("pe", lambda e, r=r: e.matmul(
                PS[2].t[:, 0:512], ones1f.t[:], wer[r].t[:], start=True, stop=True),
                reads=[ones1f, wer[r]], writes=[PS[2]])
            C.op("dve", lambda e, r=r: e.tensor_copy(wb[r].t[:], PS[2].t[:, 0:512]),
                 reads=[PS[2]], writes=[wb[r]])
            for mfp in range(MF // 2):
                bks = (0, 1) if mfp % 2 == 0 else (6, 7)

                def w1grp(e, mfp=mfp, r=r, bks=bks):
                    last = None
                    for j in range(2):
                        mf = 2 * mfp + j
                        for k in range(8):
                            last = e.matmul(
                                PS[bks[j]].t[:, 0:512],
                                w1sb.t[:, mf * 1024 + k * 128:mf * 1024 + (k + 1) * 128],
                                lnr[r].t[:, k * 512:(k + 1) * 512],
                                start=(k == 0), stop=(k == 7))
                    return last
                C.group("pe", w1grp, reads=[w1sb, lnr[r]],
                        writes=[PS[bks[0]], PS[bks[1]]])
                for j in range(2):
                    mf = 2 * mfp + j
                    C.op("act", lambda e, mf=mf, e_i=e_i, b=bks[j]: e.activation(
                        out=h_sb.t[:, mf * 512:(mf + 1) * 512], in_=PS[b].t[:, 0:512],
                        func=AF.Gelu, bias=b1sb.t[:, e_i * MF + mf:e_i * MF + mf + 1]),
                        reads=[PS[bks[j]], b1sb], writes=[h_t[mf]])
            if tb == NT - 1 and e_i + 1 < 8:
                st["load_w1"](e_i + 1)
            for dp in range(4):
                bk = (2, 3) if dp % 2 == 0 else (4, 5)
                for mfg in range(4):

                    def w2grp(e, dp=dp, mfg=mfg, bk=bk):
                        last = None
                        for mfl in range(8):
                            mf = mfg * 8 + mfl
                            for dl in range(2):
                                d = dp * 2 + dl
                                last = e.matmul(
                                    PS[bk[dl]].t[:, 0:512],
                                    w2sb.t[:, mf * 1024 + d * 128:mf * 1024 + (d + 1) * 128],
                                    h_sb.t[:, mf * 512:(mf + 1) * 512],
                                    start=(mf == 0), stop=(mf == MF - 1))
                        return last
                    C.group("pe", w2grp,
                            reads=[w2sb] + [h_t[mfg * 8 + i] for i in range(8)],
                            writes=[PS[bk[0]], PS[bk[1]]])
                for dl in range(2):
                    d = dp * 2 + dl
                    tp = tmp[(dp * 2 + dl) % 4]
                    C.op("dve", lambda e, d=d, e_i=e_i, r=r, tp=tp, b=bk[dl]:
                         e.scalar_tensor_tensor(
                            out=tp.t[:], in0=PS[b].t[:, 0:512],
                            scalar=b2sb.t[:, e_i * 8 + d:e_i * 8 + d + 1],
                            in1=wb[r].t[:], op0=ALU.add, op1=ALU.mult),
                        reads=[PS[bk[dl]], b2sb, wb[r]], writes=[tp])
                    C.dma("pool", f"ffac{dl}",
                          out_ext[d * 128:(d + 1) * 128, t0:t0 + 512],
                          tp.t[:], reads=[tp], writes=[out_db[d * NT + tb]],
                          accum_op=ALU.add)
            if tb == NT - 1 and e_i + 1 < 8:
                st["load_w2"](e_i + 1)


"""Top-level: build the full per-core program and kernel() entry."""


def build_program(cfg):
    nc = bass.Bass("TRN2", target_bir_lowering=False)
    T, D = cfg.T, cfg.D
    F32_ = F32

    u_in = nc.dram_tensor("u_in", [D, T], F32_, kind="ExternalInput").ap()
    u_halo = nc.dram_tensor("u_halo", [D, 3], F32_, kind="ExternalInput").ap()
    out_ext = nc.dram_tensor("out", [D, T], F32_, kind="ExternalOutput").ap()

    WM = [mamba_declare_weights(nc, f"Lm{l}_") for l in range(2)]
    Wffn = {
        "gate_wT": nc.dram_tensor("gate_wT", [128, 64], F32_, kind="ExternalInput").ap(),
        "gate_b": nc.dram_tensor("gate_b", [8, 1], F32_, kind="ExternalInput").ap(),
        "identf2": nc.dram_tensor("identf2", [128, 128], F32_, kind="ExternalInput").ap(),
        "w1t": nc.dram_tensor("ffn_w1t", [8 * 32, 128, 1024], F32_, kind="ExternalInput").ap(),
        "w2": nc.dram_tensor("ffn_w2", [8, 4096, 1024], F32_, kind="ExternalInput").ap(),
        "b1h": nc.dram_tensor("ffn_b1h", [128, 8 * 32], F32_, kind="ExternalInput").ap(),
        "b2c": nc.dram_tensor("ffn_b2c", [128, 64], F32_, kind="ExternalInput").ap(),
    }
    amask = nc.dram_tensor("amask", [32, 4], F32_, kind="ExternalInput").ap()
    bmask = nc.dram_tensor("bmask", [32, 4], F32_, kind="ExternalInput").ap()
    jmask = nc.dram_tensor("jmask", [32, 4], F32_, kind="ExternalInput").ap()
    hsel = nc.dram_tensor("hsel", [128, 4], F32_, kind="ExternalInput").ap()

    h0_d = nc.dram_tensor("h0_d", [D, T], F32_).ap()
    h1_d = nc.dram_tensor("h1_d", [D, T], F32_).ap()
    h2_d = nc.dram_tensor("h2_d", [D, T], F32_).ap()
    x_d = nc.dram_tensor("x_d", [D, T], F32_).ap()
    ln2_d = nc.dram_tensor("ln2_d", [D, T], BF16).ap()
    halo0_d = nc.dram_tensor("halo0_d", [D, 3], F32_).ap()
    halo1_d = nc.dram_tensor("halo1_d", [D, 3], F32_).ap()
    rout_d = nc.dram_tensor("rout_d", [8, T], F32_).ap()
    cc_s_in = [nc.dram_tensor(f"cc_s_in{l}", [65, 2048], F32_).ap() for l in range(2)]
    cc_s_out = [nc.dram_tensor(f"cc_s_out{l}", [260, 2048], F32_).ap() for l in range(2)]
    cc_h_in = nc.dram_tensor("cc_h_in", [D, 3], F32_).ap()
    cc_h_out = nc.dram_tensor("cc_h_out", [4 * D, 3], F32_).ap()
    wm_d = [nc.dram_tensor(f"wm_d{l}", [32, 4], F32_).ap() for l in range(2)]

    groups = [[0, 1, 2, 3], [4, 5, 6, 7]]

    with Ctx(nc) as C:
        def make_sin_provider(l, cc_in_db):
            def provider(C):
                cc_out_db = Buf(f"ccso{l}")
                C.op("pool", lambda e: e.collective_compute(
                    "AllGather", ALU.bypass, replica_groups=groups,
                    ins=[cc_s_in[l].opt()], outs=[cc_s_out[l].opt()]),
                    reads=[cc_in_db], writes=[cc_out_db])
                s_in = C.sbuf(f"s_in{l}", [64, 2048], F32)
                with C.phase(f"sinc{l}"):
                    pcmb = C.psum(f"sinp{l}", [128, 512], F32)
                    decj = [C.sbuf(f"sind{l}_{j}", [32, 1], F32) for j in range(4)]
                    am = C.sbuf(f"sinam{l}", [32, 4], F32)
                    bm = C.sbuf(f"sinbm{l}", [32, 4], F32)
                    jm = C.sbuf(f"sinjm{l}", [32, 4], F32)
                    eff = [C.sbuf(f"sine{l}_{j}", [32, 1], F32) for j in range(4)]
                    run = [C.sbuf(f"sinr{l}_{j}", [32, 1], F32) for j in range(4)]
                    wmat = C.sbuf(f"sinw{l}", [32, 4], F32)
                    wrow = C.sbuf(f"sinwr{l}", [1, 128], F32)
                    wbc = C.sbuf(f"sinwb{l}", [64, 128], F32)
                    o1 = C.sbuf(f"sino{l}", [1, 128], F32)
                    sjt = [C.sbuf(f"sins{l}_{i}", [64, 2048], F32) for i in range(2)]
                    C.op("dve", lambda e: e.memset(o1.t[:], 1.0), writes=[o1])
                    C.dma("sp", "sic", am.t[:], amask[:, :], writes=[am])
                    C.dma("sp", "sic", bm.t[:], bmask[:, :], writes=[bm])
                    C.dma("sp", "sic", jm.t[:], jmask[:, :], writes=[jm])
                    C.fence_stream("sic", written=[am, bm, jm])
                    for j in range(4):
                        src = cc_s_out[l][j * 65 + 64:j * 65 + 65, 0:32].rearrange("o h -> h o")
                        C.dma("sp", f"sid{j}", decj[j].t[:], src,
                              reads=[cc_out_db], writes=[decj[j]])
                        C.op("dve", lambda e, j=j: e.tensor_scalar(
                            out=eff[j].t[:], in0=decj[j].t[:], scalar1=am.t[:, j:j + 1],
                            scalar2=None, op0=ALU.mult),
                            reads=[decj[j], am], writes=[eff[j]])
                        C.op("dve", lambda e, j=j: e.tensor_scalar_add(
                            eff[j].t[:], eff[j].t[:], bm.t[:, j:j + 1]),
                            reads=[eff[j], bm], writes=[eff[j]])
                    C.op("dve", lambda e: e.memset(run[3].t[:], 1.0), writes=[run[3]])
                    for j in (2, 1, 0):
                        C.op("dve", lambda e, j=j: e.tensor_mul(
                            run[j].t[:], run[j + 1].t[:], eff[j + 1].t[:]),
                            reads=[run[j + 1], eff[j + 1]], writes=[run[j]])
                    for j in range(4):
                        C.op("dve", lambda e, j=j: e.tensor_scalar_mul(
                            wmat.t[:, j:j + 1], run[j].t[:], jm.t[:, j:j + 1]),
                            reads=[run[j], jm], writes=[wmat])
                    C.dma("sp", "siw", wm_d[l][:, :], wmat.t[:], reads=[wmat],
                          writes=[Buf(f"wmd{l}")])
                    wmd_b = Buf(f"wmd{l}")
                    wmd_b.last_w = {"siw": C.cnt["siw"]}
                    C.dma("sp", "siw2", wrow.t[:].rearrange("o (j h) -> o j h", j=4),
                          wm_d[l].rearrange("h j -> () j h"), reads=[wmd_b], writes=[wrow],
                          allow_slow_non_contiguous=True)
                    C.op("pe", lambda e: e.matmul(pcmb.t[:, 0:128], o1.t[0:1, 0:128],
                                                  wrow.t[:], start=True, stop=True),
                         reads=[o1, wrow], writes=[pcmb])
                    C.op("dve", lambda e: e.tensor_copy(wbc.t[:], pcmb.t[0:64, 0:128]),
                         reads=[pcmb], writes=[wbc])
                    C.op("dve", lambda e: e.memset(s_in.t[:], 0.0), writes=[s_in])
                    for j in range(4):
                        sj = sjt[j % 2]
                        C.dma("sp", f"sis{j % 2}", sj.t[:],
                              cc_s_out[l][j * 65:j * 65 + 64, :], reads=[cc_out_db],
                              writes=[sj])
                        for h in range(32):
                            C.op("dve", lambda e, j=j, h=h, sj=sj: e.scalar_tensor_tensor(
                                out=s_in.t[:, h * 64:(h + 1) * 64],
                                in0=sj.t[:, h * 64:(h + 1) * 64],
                                scalar=wbc.t[:, j * 32 + h:j * 32 + h + 1],
                                in1=s_in.t[:, h * 64:(h + 1) * 64],
                                op0=ALU.mult, op1=ALU.add),
                                reads=[sj, wbc, s_in], writes=[s_in])
                return s_in
            return provider

        emit_ln_pass(C, cfg, "N1", u_in, None, None, h0_d, u_halo, halo0_d, False,
                     ln_f32=True)

        with C.phase("L0w"):
            io0 = mamba_declare_scratch(nc, cfg, "Lm0_")
            cc_db0 = Buf("cc0db")
            io0.update(h_in=h0_d, h_out=h1_d, halo=halo0_d, cc_in=cc_s_in[0],
                       cc_db=cc_db0)
            emit_mamba_layer(C, cfg, "M0", WM[0], io0,
                             sin_provider=make_sin_provider(0, cc_db0))

        hx_st = {}
        halo1_db = Buf("halo1_db")

        def hx_pre(C):
            cc_h_db = Buf("cc_h_db")
            C.dma("sp", "hx0", cc_h_in[:, :], h1_d[:, T - 3:T], writes=[cc_h_db])
            hx_st.update(cc_h_db=cc_h_db)

        def hx_mid(C, g):
            if g == 2:
                cc_ho_db = Buf("cc_ho_db")
                C.op("pool", lambda e: e.collective_compute(
                    "AllGather", ALU.bypass, replica_groups=groups,
                    ins=[cc_h_in.opt()], outs=[cc_h_out.opt()]),
                    reads=[hx_st["cc_h_db"]], writes=[cc_ho_db])
                hx_st.update(cc_ho_db=cc_ho_db)
            if g != 5:
                return
            ht = C.sbuf("hxt", [128, 4], F32)
            hacc = C.sbuf("hxa", [128, 4], F32)
            hs = C.sbuf("hxs", [128, 4], F32)
            hselb = C.sbuf("hxsel", [128, 4], F32)
            cc_ho_db = hx_st["cc_ho_db"]
            C.dma("sp", "hx1", hselb.t[:], hsel[:, :], writes=[hselb])
            C.fence_stream("hx1", written=[hselb])
            for k in range(cfg.KD):
                C.op("dve", lambda e: e.memset(hacc.t[:], 0.0), writes=[hacc])
                for j in range(4):
                    C.dma("sp", "hx2", ht.t[:, 0:3],
                          cc_h_out[j * D + k * 128:j * D + (k + 1) * 128, :],
                          reads=[cc_ho_db], writes=[ht])
                    C.op("dve", lambda e, j=j: e.scalar_tensor_tensor(
                        out=hacc.t[:, 0:3], in0=ht.t[:, 0:3], scalar=hselb.t[:, j:j + 1],
                        in1=hacc.t[:, 0:3], op0=ALU.mult, op1=ALU.add),
                        reads=[ht, hselb, hacc], writes=[hacc])
                C.op("dve", lambda e: e.tensor_copy(hs.t[:, 0:3], hacc.t[:, 0:3]),
                     reads=[hacc], writes=[hs])
                C.dma("sp", "hx3", halo1_d[k * 128:(k + 1) * 128, :], hs.t[:, 0:3],
                      reads=[hs], writes=[halo1_db])

        with C.phase("L1w"):
            io1 = mamba_declare_scratch(nc, cfg, "Lm1_")
            cc_db1 = Buf("cc1db")
            io1.update(h_in=h1_d, h_out=h2_d, halo=halo1_d, cc_in=cc_s_in[1],
                       cc_db=cc_db1, halo_db=halo1_db)
            emit_mamba_layer(C, cfg, "M1", WM[1], io1,
                             sin_provider=make_sin_provider(1, cc_db1),
                             pre_hook=hx_pre, mid_hook=hx_mid)

        emit_ln_pass(C, cfg, "N2", u_in, h2_d, x_d, ln2_d, None, None, True)

        ffst = emit_ffn_prep(C, cfg, "FF", Wffn)

        emit_routing(C, cfg, "RT", Wffn, ln2_d, rout_d)

        out_db = [Buf(f"out_db{i}") for i in range(8 * (T // 512))]
        for q in range(4):
            C.dma("pool", "xcp", out_ext[:, q * (T // 4):(q + 1) * (T // 4)],
                  x_d[:, q * (T // 4):(q + 1) * (T // 4)], writes=out_db[:1])
        C.fence_stream("xcp", written=out_db)
        emit_ffn(C, cfg, "FF", Wffn, ln2_d, rout_d, out_ext, out_db, ffst)

        C.wait_all("sp", [(k, C.cnt[k]) for k in ("ffac0", "ffac1") if k in C.cnt])
    return nc


def host_prepare(inputs, core):
    b, q = core // 4, core % 4
    T = 4096
    u = inputs["L0_attn_u"][b]
    sh = np.ascontiguousarray(u[q * T:(q + 1) * T].T)
    if q == 0:
        uh = np.zeros((1024, 3), np.float32)
    else:
        uh = np.ascontiguousarray(u[q * T - 3:q * T].T)
    im = {"u_in": sh.astype(np.float32), "u_halo": uh.astype(np.float32)}
    for l in range(2):
        hw = mamba_host_weights(
            inputs["L0_attn_in_w"][l], inputs["L0_attn_conv_w"][l],
            inputs["L0_attn_conv_b"][l], inputs["L0_attn_dt_bias"][l],
            inputs["L0_attn_A_log"][l], inputs["L0_attn_Dp"][l],
            inputs["L0_attn_rms_w"][l], inputs["L0_attn_out_w"][l],
            inputs["L0_attn_norm_w"][l])
        for k, v in hw.items():
            im[f"Lm{l}_{k}"] = v
    gw = inputs["L0_ffn_gate_w"]
    im["gate_wT"] = np.ascontiguousarray(
        gw.reshape(8, 128, 8).transpose(1, 0, 2).reshape(128, 64)).astype(np.float32)
    im["gate_b"] = inputs["L0_ffn_gate_b"].reshape(8, 1).astype(np.float32)
    im["identf2"] = np.eye(128, dtype=np.float32)
    w1 = inputs["L0_ffn_w1"]
    im["ffn_w1t"] = np.ascontiguousarray(
        w1.reshape(8, 8, 128, 32, 128).transpose(0, 3, 2, 1, 4).reshape(8 * 32, 128, 8 * 128)
    ).astype(np.float32)
    im["ffn_w2"] = np.ascontiguousarray(inputs["L0_ffn_w2"]).astype(np.float32)
    b1 = inputs["L0_ffn_b1"]
    im["ffn_b1h"] = np.ascontiguousarray(
        b1.reshape(8, 32, 128).transpose(2, 0, 1).reshape(128, 256)).astype(np.float32)
    im["ffn_b2c"] = np.ascontiguousarray(
        inputs["L0_ffn_b2"].reshape(8, 8, 128).transpose(2, 0, 1).reshape(128, 64)
    ).astype(np.float32)
    r = q
    am = np.zeros((32, 4), np.float32)
    jm = np.zeros((32, 4), np.float32)
    for m in range(4):
        if m < r:
            am[:, m] = 1.0
    for j in range(4):
        if j < r:
            jm[:, j] = 1.0
    im["amask"] = am
    im["bmask"] = 1.0 - am
    im["jmask"] = jm
    hs = np.zeros((128, 4), np.float32)
    if r > 0:
        hs[:, r - 1] = 1.0
    im["hsel"] = hs
    return im


def kernel(**inputs):
    cfg = MCfg()
    nc = build_program(cfg)
    in_maps = [host_prepare(inputs, k) for k in range(8)]
    res = run_bass_kernel_spmd(nc, in_maps, core_ids=list(range(8)))
    out = np.empty((2, 16384, 1024), np.float32)
    for k in range(8):
        b, q = k // 4, k % 4
        out[b, q * 4096:(q + 1) * 4096, :] = res.results[k]["out"].T
    return out



# version 16 (final)
# speedup vs baseline: 1.0046x; 1.0046x over previous
import numpy as np
from contextlib import ExitStack, contextmanager

import concourse.bass as bass
import concourse.mybir as mybir
from concourse.bass_utils import run_bass_kernel_spmd

"""Dependency-tracked raw-Bass emission helpers (hand-rolled, no Tile).

Every engine gets one progress semaphore. Every tracked instruction increments
its engine's semaphore by 1 (DMA: stream semaphores, +16). Buffers (Buf) track
last-writer and reader tokens; emission inserts the minimal wait_ge set for
RAW/WAR/WAW edges, deduplicated via per-consumer watermarks.

PE matmul accumulation groups get a single inc on the last matmul (PE completes
in program order, so that inc covers the whole group).
"""


F32 = mybir.dt.float32
BF16 = mybir.dt.bfloat16
AF = mybir.ActivationFunctionType
ALU = mybir.AluOpType
AX = mybir.AxisListType

SECTION = {"pe": "tensor", "act": "scalar", "dve": "vector", "pool": "gpsimd", "sp": "sync"}


class Buf:
    __slots__ = ("name", "t", "last_w", "readers")

    def __init__(self, name, t=None):
        self.name = name
        self.t = t
        self.last_w = {}
        self.readers = []


class Ctx:
    def __init__(self, nc):
        self.nc = nc
        self.stack = ExitStack()
        self.block = None
        self.sems = {}
        self.cnt = {}
        self.wm = {e: {} for e in SECTION}

    def __enter__(self):
        self.stack.__enter__()
        self.sem_stack = self.stack
        for e in SECTION:
            self._mk_sem(e)
        self.block = self.stack.enter_context(self.nc.Block())
        return self

    def __exit__(self, *a):
        return self.stack.__exit__(*a)

    def _mk_sem(self, key):
        stack = getattr(self, "sem_stack", self.stack)
        self.sems[key] = stack.enter_context(self.nc.semaphore("s_" + key))
        self.cnt[key] = 0

    def stream(self, key):
        if key not in self.sems:
            self._mk_sem(key)
        return key

    def sbuf(self, name, shape, dtype=F32):
        t = self.stack.enter_context(self.nc.sbuf_tensor(name, list(shape), dtype))
        return Buf(name, t)

    def psum(self, name, shape, dtype=F32):
        t = self.stack.enter_context(self.nc.psum_tensor(name, list(shape), dtype))
        return Buf(name, t)

    def dram(self, name, shape, dtype=F32, kind="Internal"):
        t = self.nc.dram_tensor(name, list(shape), dtype, kind=kind)
        return Buf(name, t)

    def _needs(self, reads, writes):
        toks = {}

        def add(tok):
            if tok is None:
                return
            k, c = tok
            if toks.get(k, 0) < c:
                toks[k] = c

        for b in reads:
            for kc in b.last_w.items():
                add(kc)
        for b in writes:
            for kc in b.last_w.items():
                add(kc)
            for t in b.readers:
                add(t)
        return toks

    def _emit_waits(self, eng, e, toks):
        wm = self.wm[eng]
        for k in sorted(toks):
            c = toks[k]
            if wm.get(k, 0) < c:
                if not (eng == "pe" and k == "pe"):
                    e.wait_ge(self.sems[k], c)
                wm[k] = c

    def _update(self, reads, writes, tok):
        k, c = tok
        for b in reads:
            b.readers.append(tok)
        for b in writes:
            if b.last_w.get(k, 0) < c:
                b.last_w[k] = c
            b.readers = []

    def fence_stream(self, skey, written=(), read=()):
        tot = self.cnt.get(skey, 0)
        for b in written:
            b.last_w = {skey: tot}
        for b in read:
            b.readers = [(skey, tot)]

    def op(self, eng, fn, reads=(), writes=(), extra_waits=()):
        toks = self._needs(reads, writes)
        for tok in extra_waits:
            k, c = tok
            if toks.get(k, 0) < c:
                toks[k] = c
        sem = self.sems[eng]

        def body(e):
            self._emit_waits(eng, e, toks)
            ins = fn(e)
            ins.then_inc(sem, 1)

        getattr(self.block, SECTION[eng])(body)
        self.cnt[eng] += 1
        self._update(reads, writes, (eng, self.cnt[eng]))
        return (eng, self.cnt[eng])

    def group(self, eng, fn, reads=(), writes=(), extra_waits=()):
        toks = self._needs(reads, writes)
        for tok in extra_waits:
            k, c = tok
            if toks.get(k, 0) < c:
                toks[k] = c
        sem = self.sems[eng]

        def body(e):
            self._emit_waits(eng, e, toks)
            last = fn(e)
            last.then_inc(sem, 1)

        getattr(self.block, SECTION[eng])(body)
        self.cnt[eng] += 1
        self._update(reads, writes, (eng, self.cnt[eng]))
        return (eng, self.cnt[eng])

    def dma(self, qeng, skey, out_ap, in_ap, reads=(), writes=(), transpose=False, **kw):
        self.stream(skey)
        toks = self._needs(reads, writes)
        sem = self.sems[skey]

        def body(e):
            self._emit_waits(qeng, e, toks)
            if transpose:
                ins = e.dma_start_transpose(out_ap, in_ap)
            else:
                ins = e.dma_start(out=out_ap, in_=in_ap, **kw)
            ins.then_inc(sem, 16)

        getattr(self.block, SECTION[qeng])(body)
        self.cnt[skey] += 16
        self._update(reads, writes, (skey, self.cnt[skey]))
        return (skey, self.cnt[skey])

    def cc(self, skey, fn, reads=(), writes=()):
        self.stream(skey)
        toks = self._needs(reads, writes)
        sem = self.sems[skey]

        def body(e):
            self._emit_waits("pool", e, toks)
            ins = fn(e)
            ins.then_inc(sem, 16)

        self.block.gpsimd(body)
        self.cnt[skey] += 16
        self._update(reads, writes, (skey, self.cnt[skey]))
        return (skey, self.cnt[skey])

    def raw(self, eng, fn):
        getattr(self.block, SECTION[eng])(fn)

    def act_table(self, idx):
        def body(e):
            e.add_instruction(mybir.InstLoadActFuncSet(
                name=self.nc.get_next_instruction_name(),
                act_func_set_id=idx, ins=[], outs=[]))
        self.raw("act", body)

    def wait_all(self, eng, toks):
        def body(e):
            self._emit_waits(eng, e, dict(toks))
        getattr(self.block, SECTION[eng])(body)

    def barrier_token(self, eng):
        return (eng, self.cnt[eng])

    def barrier(self):
        toks = {k: c for k, c in self.cnt.items() if c > 0}
        for eng in SECTION:
            self.wait_all(eng, toks.items())

    from contextlib import contextmanager

    @contextmanager
    def phase(self, name):
        from contextlib import ExitStack
        outer = self.stack
        inner = ExitStack()
        self.stack = inner
        try:
            yield
        finally:
            self.stack = outer
            self.barrier()
            inner.close()


"""Mamba-2 (SSD) single-layer emitter for raw Bass. CT layout, CS=256 chunks.

h_out = h_in + out_proj(ssd_block(rmsnorm(h_in) * norm_w))

Conventions:
  - ncum = NEGATED running cumsum of dA (positive, grows); cum = -ncum.
  - dt_s / nc_s: (32, T) fp32 band tensors (dt and ncum).
  - per-chunk transposed cols: A2 dcwT (128,128): col half*64 + h = W,
    half*64 + 32 + h = dec; B dcwTB: col half*64 + h = dt, +32+h = ncum.
"""


class MCfg:
    T = 4096
    D = 1024
    DI = 2048
    H = 32
    CS = 256
    NCH = T // CS
    NTG = T // 512
    KD = D // 128
    MI = 34
    EPS = 1e-5


def emit_mamba_layer(C: Ctx, cfg, pfx, W, io, sin_provider=None,
                     pre_hook=None, mid_hook=None):
    T, CS, NCH, H, KD, MI = cfg.T, cfg.CS, cfg.NCH, cfg.H, cfg.KD, cfg.MI
    if pre_hook is not None:
        pre_hook(C)

    dt_s = C.sbuf(pfx + "dt_s", [32, T], F32)
    nc_s = C.sbuf(pfx + "nc_s", [32, T], F32)
    b_ct = C.sbuf(pfx + "b_ct", [64, T], BF16)
    c0_ct = C.sbuf(pfx + "c0_ct", [64, T], BF16)
    nclprevT = C.sbuf(pfx + "nclprevT", [128, 512], F32)
    carry = C.sbuf(pfx + "carry", [32, 1], F32)
    negA = C.sbuf(pfx + "negA", [32, 1], F32)
    dtb = C.sbuf(pfx + "dtb", [32, 1], F32)
    convw = C.sbuf(pfx + "convw", [128, 17 * 4], F32)
    convb = C.sbuf(pfx + "convb", [128, 17], F32)
    dexp = C.sbuf(pfx + "dexp", [128, 16], F32)
    rmsw = C.sbuf(pfx + "rmsw", [128, 16], F32)
    normw = C.sbuf(pfx + "normw", [128, 8], F32)
    ident = C.sbuf(pfx + "ident", [128, 128], BF16)
    identf = C.sbuf(pfx + "identf", [128, 128], F32)
    ones1f = C.sbuf(pfx + "ones1f", [1, 128], F32)
    onescol = C.sbuf(pfx + "onescol", [128, 1], F32)
    cstc = C.sbuf(pfx + "cstc", [128, 2], F32)
    cl_dram = C.dram(pfx + "cl_dram", [32, 16], F32)

    for nm, bb in [("dt_bias", dtb), ("conv_w", convw), ("conv_b", convb),
                   ("Dexp", dexp), ("rms_w", rmsw), ("norm_w", normw), ("identf", identf)]:
        C.dma("sp", "cst", bb.t[:], W[nm][:, :], writes=[bb])
    C.dma("pool", "cst2", ident.t[:], W["ident"][:, :], writes=[ident])
    C.fence_stream("cst", written=[dtb, convw, convb, dexp, rmsw, normw, identf])
    C.fence_stream("cst2", written=[ident])
    C.op("dve", lambda e: e.memset(ones1f.t[:], 1.0), writes=[ones1f])
    C.op("dve", lambda e: e.memset(onescol.t[:], 1.0), writes=[onescol])
    C.op("dve", lambda e: e.memset(cstc.t[:, 0:1], cfg.EPS), writes=[cstc])
    C.op("dve", lambda e: e.memset(cstc.t[:, 1:2], 1.0), writes=[cstc])
    algb = C.sbuf(pfx + "algb", [32, 1], F32)
    C.dma("sp", "cst3", algb.t[:], W["A_log"][:, :], writes=[algb])
    C.act_table(6)
    C.op("act", lambda e: e.activation(out=negA.t[:], in_=algb.t[:], func=AF.Exp),
         reads=[algb], writes=[negA])
    C.op("dve", lambda e: e.memset(carry.t[:], 0.0), writes=[carry])

    sz_d, xbc_d = io["sz_d"], io["xbc_d"]
    xct_d, xtn_d, s_d = io["xct_d"], io["xtn_d"], io["s_d"]
    h_in, h_out, halo = io["h_in"], io["h_out"], io["halo"]
    sz_db, xbc_db = Buf(pfx + "sz_db"), Buf(pfx + "xbc_db")
    xct_db = [Buf(pfx + f"xct_db{c}") for c in range(NCH)]
    xtn_db = [Buf(pfx + f"xtn_db{c}") for c in range(NCH)]
    s_db = [Buf(pfx + f"s_db{c}") for c in range(NCH)]
    cl_db = Buf(pfx + "cl_db")

    hout_db = [[Buf(pfx + f"hout_db{c}_{m}") for m in range(8)] for c in range(NCH)]
    _all_hout = [b for row in hout_db for b in row]
    for cq in range(4):
        C.dma("pool", "hcp", h_out[:, cq * (T // 4):(cq + 1) * (T // 4)],
              h_in[:, cq * (T // 4):(cq + 1) * (T // 4)], writes=_all_hout[:1])
    C.fence_stream("hcp", written=_all_hout)

    with C.phase(pfx + "A1"):
        PP = [C.psum(pfx + f"ppA{i}", [128, 512], F32) for i in range(4)]
        PA = [C.psum(pfx + f"ppC{i}", [128, 512], F32) for i in range(2)]
        PROT = [PP[0], PP[1], PA[0], PA[1]]
        w_in = [C.sbuf(pfx + f"win{k}", [128, 4256], BF16) for k in range(KD)]
        for k in range(KD):
            C.dma("pool", "winld", w_in[k].t[:],
                  W["w_inT"][k * 128:(k + 1) * 128, :], writes=[w_in[k]])
        C.fence_stream("winld", written=w_in)

        hbuf = [[C.sbuf(pfx + f"hb{r}_{k}", [128, 512], F32) for k in range(KD)] for r in range(2)]
        hsq = [C.sbuf(pfx + f"hsq{k}", [128, 512], F32) for k in range(KD)]
        vbf = [[C.sbuf(pfx + f"vb{r}_{k}", [128, 512], BF16) for k in range(KD)] for r in range(2)]
        rstd = C.sbuf(pfx + "rstd", [1, 512], F32)
        zs = [C.sbuf(pfx + f"zs{q}", [128, 512], BF16) for q in range(4)]
        xs = [C.sbuf(pfx + f"xs{q}", [128, 512], BF16) for q in range(4)]
        sp1 = [C.sbuf(pfx + f"sp1_{i}", [32, 512], F32) for i in range(3)]
        nda = C.sbuf(pfx + "nda", [32, 512], F32)
        zero32 = C.sbuf(pfx + "zero32", [32, 512], F32)
        C.op("dve", lambda e: e.memset(zero32.t[:], 0.0), writes=[zero32])

        hh = [C.sbuf(pfx + f"hh{k}", [128, 4], F32) for k in range(KD)]
        vh = [C.sbuf(pfx + f"vh{k}", [128, 4], BF16) for k in range(KD)]
        hhs = C.sbuf(pfx + "hhs", [128, 4], F32)
        rsth = C.sbuf(pfx + "rsth", [1, 4], F32)
        xh = C.sbuf(pfx + "xh", [128, 4], BF16)
        for g in range(cfg.NTG):
            if mid_hook is not None:
                mid_hook(C, g)
            r = g % 2
            t0 = g * 512
            for k in range(KD):
                C.dma("pool", f"hld{r}", hbuf[r][k].t[:],
                      h_in[k * 128:(k + 1) * 128, t0:t0 + 512], writes=[hbuf[r][k]])
            C.fence_stream(f"hld{r}", written=hbuf[r])
            for k in range(KD):
                C.op("act", lambda e, k=k, r=r: e.activation(out=hsq[k].t[:], in_=hbuf[r][k].t[:],
                                                             func=AF.Square),
                     reads=[hbuf[r][k]], writes=[hsq[k]])
                C.op("pe", lambda e, k=k: e.matmul(PP[2].t[0:1, :], onescol.t[:], hsq[k].t[:],
                                                   start=(k == 0), stop=(k == KD - 1)),
                     reads=[hsq[k], onescol], writes=[PP[2]])
            C.op("act", lambda e: e.activation(out=rstd.t[:], in_=PP[2].t[0:1, 0:512], func=AF.Ln,
                                               scale=1.0 / cfg.D, bias=cstc.t[0:1, 0:1]),
                 reads=[PP[2], cstc], writes=[rstd])
            C.op("act", lambda e: e.activation(out=rstd.t[:], in_=rstd.t[:], func=AF.Exp,
                                               scale=-0.5),
                 reads=[rstd], writes=[rstd])
            C.op("pe", lambda e: e.matmul(PP[2].t[:, :], ones1f.t[:], rstd.t[:],
                                          start=True, stop=True),
                 reads=[rstd, ones1f], writes=[PP[2]])
            for k in range(KD):
                C.op("dve", lambda e, k=k, r=r: e.scalar_tensor_tensor(
                    out=vbf[r][k].t[:], in0=hbuf[r][k].t[:], scalar=normw.t[:, k:k + 1],
                    in1=PP[2].t[:, :], op0=ALU.mult, op1=ALU.mult),
                    reads=[hbuf[r][k], normw, PP[2]], writes=[vbf[r][k]])
            for m in range(MI):
                pp = PROT[m % 4]
                rows = 128 if m < 33 else 32

                def pgrp(e, m=m, r=r, pp=pp, rows=rows):
                    last = None
                    for k in range(KD):
                        last = e.matmul(pp.t[0:rows, 0:512],
                                        w_in[k].t[:, m * 128:m * 128 + rows],
                                        vbf[r][k].t[:],
                                        start=(k == 0), stop=(k == KD - 1))
                    return last
                C.group("pe", pgrp, reads=w_in + vbf[r], writes=[pp])

                if m < 16:
                    if m == 0:
                        C.act_table(18)
                    q = m % 4
                    C.op("act", lambda e, q=q, pp=pp: e.activation(
                        out=zs[q].t[:], in_=pp.t[:, 0:512], func=AF.Silu),
                        reads=[pp], writes=[zs[q]])
                    dst = sz_d[2 * g:2 * g + 2, :, m * CS:(m + 1) * CS].rearrange("c p t -> p c t")
                    src = zs[q].t[:].rearrange("p (c t) -> p c t", c=2)
                    C.dma("sp", f"szo{q}", dst, src, reads=[zs[q]], writes=[sz_db])
                elif m < 33:
                    q = m % 4
                    C.op("act", lambda e, q=q, pp=pp: e.activation(
                        out=xs[q].t[:], in_=pp.t[:, 0:512], func=AF.Copy),
                        reads=[pp], writes=[xs[q]])
                    C.dma("act", f"xbo{q}", xbc_d[m - 16, :, 3 + t0:3 + t0 + 512],
                          xs[q].t[:], reads=[xs[q]], writes=[xbc_db])
                else:
                    C.act_table(6)
                    C.op("act", lambda e, pp=pp: e.activation(
                        out=sp1[0].t[:], in_=pp.t[0:32, 0:512], func=AF.Abs, bias=dtb.t[:]),
                        reads=[pp, dtb], writes=[sp1[0]])
                    C.op("act", lambda e: e.activation(
                        out=sp1[1].t[:], in_=sp1[0].t[:], func=AF.Exp, scale=-1.0),
                        reads=[sp1[0]], writes=[sp1[1]])
                    C.op("act", lambda e: e.activation(
                        out=sp1[1].t[:], in_=sp1[1].t[:], func=AF.Ln,
                        bias=cstc.t[0:32, 1:2]),
                        reads=[sp1[1], cstc], writes=[sp1[1]])
                    C.op("act", lambda e, pp=pp: e.activation(
                        out=sp1[2].t[:], in_=pp.t[0:32, 0:512], func=AF.Relu, bias=dtb.t[:]),
                        reads=[pp, dtb], writes=[sp1[2]])
                    C.op("dve", lambda e, t0=t0: e.tensor_add(
                        dt_s.t[:, t0:t0 + 512], sp1[1].t[:], sp1[2].t[:]),
                        reads=[sp1[1], sp1[2]], writes=[dt_s])
                    C.op("dve", lambda e, t0=t0: e.tensor_scalar_mul(
                        nda.t[:], dt_s.t[:, t0:t0 + 512], negA.t[:]),
                        reads=[dt_s, negA], writes=[nda])
                    C.op("dve", lambda e, t0=t0: e.tensor_tensor_scan(
                        out=nc_s.t[:, t0:t0 + 512], data0=nda.t[:], data1=zero32.t[:],
                        initial=carry.t[:], op0=ALU.add, op1=ALU.add),
                        reads=[nda, zero32, carry], writes=[nc_s])
                    C.op("dve", lambda e, t0=t0: e.tensor_copy(
                        carry.t[:], nc_s.t[:, t0 + 511:t0 + 512]),
                        reads=[nc_s], writes=[carry])

        halo_rd = [io["halo_db"]] if "halo_db" in io else []
        for k in range(KD):
            C.dma("sp", "hl", hh[k].t[:, 0:3], halo[k * 128:(k + 1) * 128, :],
                  reads=halo_rd, writes=[hh[k]])
        C.fence_stream("hl", written=hh)
        for k in range(KD):
            C.op("act", lambda e, k=k: e.activation(out=hhs.t[:, 0:3], in_=hh[k].t[:, 0:3],
                                                    func=AF.Square),
                 reads=[hh[k]], writes=[hhs])
            C.op("pe", lambda e, k=k: e.matmul(PP[2].t[0:1, 0:3], onescol.t[:], hhs.t[:, 0:3],
                                               start=(k == 0), stop=(k == KD - 1)),
                 reads=[hhs, onescol], writes=[PP[2]])
        C.op("act", lambda e: e.activation(out=rsth.t[:, 0:3], in_=PP[2].t[0:1, 0:3], func=AF.Ln,
                                           scale=1.0 / cfg.D, bias=cstc.t[0:1, 0:1]),
             reads=[PP[2], cstc], writes=[rsth])
        C.op("act", lambda e: e.activation(out=rsth.t[:, 0:3], in_=rsth.t[:, 0:3], func=AF.Exp,
                                           scale=-0.5),
             reads=[rsth], writes=[rsth])
        C.op("pe", lambda e: e.matmul(PP[2].t[:, 0:3], ones1f.t[:], rsth.t[:, 0:3],
                                      start=True, stop=True),
             reads=[rsth, ones1f], writes=[PP[2]])
        for k in range(KD):
            C.op("dve", lambda e, k=k: e.scalar_tensor_tensor(
                out=vh[k].t[:, 0:3], in0=hh[k].t[:, 0:3], scalar=normw.t[:, k:k + 1],
                in1=PP[2].t[:, 0:3], op0=ALU.mult, op1=ALU.mult),
                reads=[hh[k], normw, PP[2]], writes=[vh[k]])
        for m in range(16, 33):
            def hgrp(e, m=m):
                last = None
                for k in range(KD):
                    last = e.matmul(PP[3].t[:, 0:3],
                                    w_in[k].t[:, m * 128:(m + 1) * 128],
                                    vh[k].t[:, 0:3],
                                    start=(k == 0), stop=(k == KD - 1))
                return last
            C.group("pe", hgrp, reads=w_in + vh, writes=[PP[3]])
            C.op("act", lambda e: e.activation(out=xh.t[:, 0:3], in_=PP[3].t[:, 0:3],
                                               func=AF.Copy),
                 reads=[PP[3]], writes=[xh])
            C.dma("sp", "hxo", xbc_d[m - 16, :, 0:3], xh.t[:, 0:3], reads=[xh],
                  writes=[xbc_db])

        cl_sb = C.sbuf(pfx + "cl_sb", [32, 16], F32)
        src = nc_s.t[:].rearrange("p (c t) -> p c t", c=NCH)[:, :, CS - 1:CS]
        C.op("dve", lambda e: e.tensor_copy(cl_sb.t[:, :], src), reads=[nc_s], writes=[cl_sb])
        C.dma("sp", "clo", cl_dram.t[:, :], cl_sb.t[:], reads=[cl_sb], writes=[cl_db])
        nclrow = C.sbuf(pfx + "nclrow", [1, 512], F32)
        C.op("dve", lambda e: e.memset(nclrow.t[:], 0.0), writes=[nclrow])
        dstrow = nclrow.t[:].rearrange("o (h c) -> o h c", h=H)[:, :, 1:16]
        C.dma("sp", "cli", dstrow,
              cl_dram.t[:, 0:15].rearrange("h c -> () h c"), reads=[cl_db], writes=[nclrow])
        C.op("pe", lambda e: e.matmul(PP[3].t[:, 0:512], ones1f.t[:], nclrow.t[:],
                                      start=True, stop=True),
             reads=[nclrow, ones1f], writes=[PP[3]])
        C.op("dve", lambda e: e.tensor_copy(nclprevT.t[:], PP[3].t[:, 0:512]),
             reads=[PP[3]], writes=[nclprevT])

    with C.phase(pfx + "A2"):
        PT = [C.psum(pfx + f"ptA{i}", [128, 1024], BF16) for i in range(4)]
        PF = [C.psum(pfx + f"pfA{i}", [128, 512], F32) for i in range(4)]
        xr = [C.sbuf(pfx + f"xr{r}", [128, 17 * (2 * CS + 3)], BF16) for r in range(2)]
        ptslot = [[Buf(pfx + f"pts{h}_{s}") for s in range(4)] for h in range(2)]
        pfslot = [Buf(pfx + f"pfs{s}") for s in range(4)]
        cacc = [C.sbuf(pfx + f"cacc{i}", [128, 2 * CS], F32) for i in range(2)]
        bctmp = C.sbuf(pfx + "bctmp", [128, 2 * CS], BF16)
        csil = [C.sbuf(pfx + f"csil{r}", [128, 16 * 2 * CS], BF16) for r in range(2)]
        cpool = C.sbuf(pfx + "cpool", [128, CS], F32)
        xtns = [C.sbuf(pfx + f"xtns{h}", [128, cfg.DI], BF16) for h in range(2)]
        btn = C.sbuf(pfx + "btn", [128, 128], BF16)
        xwh = [C.sbuf(pfx + f"xwh{i}", [128, 64], BF16) for i in range(8)]
        stslot = [Buf(pfx + f"stsl{j}") for j in range(4)]
        w_s = C.sbuf(pfx + "w_s", [32, CS], F32)
        dec_s = C.sbuf(pfx + "dec_s", [32, CS], F32)
        dcwT = C.sbuf(pfx + "dcwT", [128, 128], F32)
        sping = [C.sbuf(pfx + f"sping{r}", [64, H * 64], F32) for r in range(2)]
        nlc = C.sbuf(pfx + "nlc", [32, 2], F32)
        C.op("dve", lambda e: e.memset(sping[1].t[:], 0.0), writes=[sping[1]])

        W2C = 2 * CS + 3

        def load_xbc(cp, r):
            for half in range(2):
                ms, me = (0, 9) if half == 0 else (9, 17)
                src = xbc_d[ms:me, :, cp * 2 * CS:cp * 2 * CS + W2C].rearrange(
                    "m p t -> p m t")
                dst = xr[r].t[:, ms * W2C:me * W2C].rearrange(
                    "p (m t) -> p m t", m=me - ms)
                C.dma("pool", f"xbl{r}{half}", dst, src, reads=[xbc_db], writes=[xr[r]])

        load_xbc(0, 0)
        for c in range(NCH):
            r = (c // 2) % 2
            tc = c * CS
            if c % 2 == 0 and c + 2 < NCH:
                load_xbc(c // 2 + 1, 1 - r)
            C.op("dve", lambda e, tc=tc: e.tensor_scalar_mul(
                nlc.t[:, 0:1], nc_s.t[:, tc + CS - 1:tc + CS], -1.0),
                reads=[nc_s], writes=[nlc])
            C.op("act", lambda e, tc=tc: e.activation(
                out=w_s.t[:], in_=nc_s.t[:, tc:tc + CS], func=AF.Exp,
                scale=1.0, bias=nlc.t[:, 0:1]),
                reads=[nc_s, nlc], writes=[w_s])
            C.op("dve", lambda e, tc=tc: e.tensor_mul(
                w_s.t[:], w_s.t[:], dt_s.t[:, tc:tc + CS]),
                reads=[w_s, dt_s], writes=[w_s])
            if c == 0:
                C.op("act", lambda e: e.activation(out=nlc.t[:, 1:2], in_=nlc.t[:, 0:1],
                                                   func=AF.Exp, scale=1.0),
                     reads=[nlc], writes=[nlc])
            else:
                C.op("dve", lambda e, tc=tc: e.tensor_add(
                    nlc.t[:, 1:2], nc_s.t[:, tc - 1:tc], nlc.t[:, 0:1]),
                    reads=[nc_s, nlc], writes=[nlc])
                C.op("act", lambda e: e.activation(out=nlc.t[:, 1:2], in_=nlc.t[:, 1:2],
                                                   func=AF.Exp),
                     reads=[nlc], writes=[nlc])
            C.op("dve", lambda e: e.tensor_scalar(
                out=dec_s.t[:], in0=w_s.t[:], scalar1=0.0, scalar2=nlc.t[:, 1:2],
                op0=ALU.mult, op1=ALU.add),
                reads=[w_s, nlc], writes=[dec_s])
            if c % 2 == 0:
                for m in range(17):
                    if m == 0:
                        C.act_table(18)
                    base = m * W2C
                    cc = cacc[m % 2]
                    C.op("dve", lambda e, base=base, m=m, r=r, cc=cc: e.tensor_scalar_mul(
                        cc.t[:], xr[r].t[:, base:base + 2 * CS], convw.t[:, m * 4:m * 4 + 1]),
                        reads=[xr[r], convw], writes=[cc])
                    for w in range(1, 4):
                        C.op("dve", lambda e, base=base, m=m, r=r, w=w, cc=cc:
                             e.scalar_tensor_tensor(
                                out=cc.t[:], in0=xr[r].t[:, base + w:base + w + 2 * CS],
                                scalar=convw.t[:, m * 4 + w:m * 4 + w + 1], in1=cc.t[:],
                                op0=ALU.mult, op1=ALU.add),
                             reads=[xr[r], convw, cc], writes=[cc])
                    if m < 16:
                        C.op("act", lambda e, m=m, r=r, cc=cc: e.activation(
                            out=csil[r].t[:, m * 2 * CS:(m + 1) * 2 * CS], in_=cc.t[:],
                            func=AF.Silu, bias=convb.t[:, m:m + 1]),
                            reads=[cc, convb], writes=[csil[r]])
                    else:
                        C.op("act", lambda e, tc=tc, cc=cc: e.activation(
                            out=bctmp.t[:], in_=cc.t[:], func=AF.Silu,
                            bias=convb.t[:, 16:17]),
                            reads=[cc, convb], writes=[bctmp])
                        C.op("act", lambda e, tc=tc: e.copy(
                            out=b_ct.t[:, tc:tc + 2 * CS], in_=bctmp.t[0:64, :]),
                            reads=[bctmp], writes=[b_ct])
                        C.op("pool", lambda e, tc=tc: e.tensor_copy(
                            c0_ct.t[:, tc:tc + 2 * CS], bctmp.t[64:128, :]),
                            reads=[bctmp], writes=[c0_ct])

            C.act_table(6)
            xsrc = csil[r].t[:].rearrange("p (m t2) -> p m t2", m=16)[
                :, :, (c % 2) * CS:(c % 2 + 1) * CS]
            C.dma("act", f"xco{(c // 2) % 2}{c % 2}",
                  xct_d[c, :, :].rearrange("p (m t) -> p m t", m=16),
                  xsrc, reads=[csil[r]], writes=[xct_db[c]])
            for half in range(2):
                for m in range(16):
                    pt = PT[half * 2 + m % 2]
                    pb = ptslot[half][m % 2]
                    off = m * 2 * CS + (c % 2) * CS + half * 128
                    C.op("pe", lambda e, off=off, r=r, pt=pt: e.transpose(
                        pt.t[:, 0:128],
                        csil[r].t[:, off:off + 128],
                        ident.t[:]),
                        reads=[csil[r]], writes=[pb])
                    C.op("act", lambda e, m=m, half=half, pt=pt: e.copy(
                        out=xtns[half].t[:, m * 128:(m + 1) * 128], in_=pt.t[:, 0:128]),
                        reads=[pb], writes=[xtns[half]])
                C.dma("sp", f"xto{half}", xtn_d[c, half, :, :], xtns[half].t[:],
                      reads=[xtns[half]], writes=[xtn_db[c]])
            for half in range(2):
                for bi, band in ((0, w_s), (1, dec_s)):
                    sl = half * 2 + bi
                    pf = PF[sl % 2]
                    pb = pfslot[sl % 2]
                    C.op("pe", lambda e, half=half, band=band, pf=pf: e.transpose(
                        pf.t[0:128, 0:32], band.t[:, half * 128:half * 128 + 128],
                        identf.t[0:32, 0:32]),
                        reads=[band, identf], writes=[pb])
                    C.op("dve", lambda e, half=half, bi=bi, pf=pf: e.tensor_copy(
                        dcwT.t[:, half * 64 + bi * 32:half * 64 + bi * 32 + 32],
                        pf.t[:, 0:32]),
                        reads=[pb], writes=[dcwT])
            for half in range(2):
                pt = PT[half * 2]
                pb = ptslot[half][0]
                C.op("pe", lambda e, tc=tc, half=half, pt=pt: e.transpose(
                    pt.t[:, 0:64],
                    b_ct.t[:, tc + half * 128:tc + half * 128 + 128],
                    ident.t[0:64, 0:64]),
                    reads=[b_ct, ident], writes=[pb])
                C.op("act", lambda e, half=half, pt=pt: e.copy(
                    out=btn.t[:, half * 64:half * 64 + 64], in_=pt.t[:, 0:64]),
                    reads=[pb], writes=[btn])
            def emit_xw(h):
                for half in range(2):
                    xw2 = xwh[(h % 4) * 2 + half]
                    C.op("dve", lambda e, h=h, half=half, xw2=xw2: e.tensor_scalar_mul(
                        xw2.t[:], xtns[half].t[:, h * 64:(h + 1) * 64],
                        dcwT.t[:, half * 64 + h:half * 64 + h + 1]),
                        reads=[xtns[half], dcwT], writes=[xw2])
            emit_xw(0)
            emit_xw(1)
            for h in range(H):
                j = h % 4
                stb = PF[2 + j // 2]
                sco = (j % 2) * 64
                sl = stslot[j]
                xpair = [xwh[j * 2], xwh[j * 2 + 1]]

                def sgrp(e, stb=stb, sco=sco, xpair=xpair):
                    last = None
                    for half in range(2):
                        last = e.matmul(stb.t[0:64, sco:sco + 64],
                                        btn.t[:, half * 64:half * 64 + 64],
                                        xpair[half].t[:],
                                        start=(half == 0), stop=(half == 1))
                    return last
                C.group("pe", sgrp, reads=[btn] + xpair, writes=[sl])
                if h + 2 < H:
                    emit_xw(h + 2)
                C.op("dve", lambda e, h=h, r=r, stb=stb, sco=sco, sl=sl:
                     e.scalar_tensor_tensor(
                        out=sping[r].t[:, h * 64:(h + 1) * 64],
                        in0=sping[1 - r].t[:, h * 64:(h + 1) * 64],
                        scalar=dcwT.t[0:64, 32 + h:32 + h + 1],
                        in1=stb.t[0:64, sco:sco + 64], op0=ALU.mult, op1=ALU.add),
                        reads=[sping[1 - r], dcwT, sl], writes=[sping[r]])
            C.dma("sp", f"sso{r}", s_d[c, :, :], sping[r].t[:], reads=[sping[r]],
                  writes=[s_db[c]])

        if "cc_in" in io:
            cc_in = io["cc_in"]
            cc_db = io["cc_db"]
            dtot = C.sbuf(pfx + "dtot", [32, 1], F32)
            zrow = C.sbuf(pfx + "zrow", [1, 2048], F32)
            C.op("dve", lambda e: e.memset(zrow.t[:], 0.0), writes=[zrow])
            C.op("act", lambda e: e.activation(out=dtot.t[:], in_=nc_s.t[:, T - 1:T],
                                               func=AF.Exp, scale=-1.0),
                 reads=[nc_s], writes=[dtot])
            C.dma("sp", "ccp", cc_in[0:64, :], sping[(NCH - 1) % 2].t[:],
                  reads=[sping[(NCH - 1) % 2]], writes=[cc_db])
            C.dma("sp", "ccp2", cc_in[64:65, :], zrow.t[:], reads=[zrow],
                  writes=[cc_db])
            C.dma("sp", "ccp3", cc_in[64:65, 0:32].rearrange("o h -> h o"),
                  dtot.t[:], reads=[dtot], writes=[cc_db])

    s_in = sin_provider(C) if sin_provider is not None else None

    with C.phase(pfx + "B"):
        PB = [C.psum(pfx + f"pbB{i}", [128, 512], F32) for i in range(8)]
        onehot = C.sbuf(pfx + "onehot", [32, H * 128], F32)
        mask = [C.sbuf(pfx + f"mask{j}", [128, 256], BF16) for j in range(2)]
        for nm, bb in [("onehot", onehot), ("mask0", mask[0]), ("mask1", mask[1])]:
            C.dma("pool", "cstB", bb.t[:], W[nm][:, :], writes=[bb])
        C.fence_stream("cstB", written=[onehot, mask[0], mask[1]])

        eprefT = None
        if s_in is not None:
            eprefT = C.sbuf(pfx + "eprefT", [64, 512], F32)
            C.op("act", lambda e: e.activation(out=eprefT.t[:], in_=nclprevT.t[0:64, :],
                                               func=AF.Exp, scale=-1.0),
                 reads=[nclprevT], writes=[eprefT])

        out_w = [C.sbuf(pfx + f"wout{k}", [128, cfg.D], BF16) for k in range(16)]
        for k in range(16):
            C.dma("pool", "wold", out_w[k].t[:], W["out_wT"][k * 128:(k + 1) * 128, :],
                  writes=[out_w[k]])
        C.fence_stream("wold", written=out_w)
        for k in range(16):
            C.op("dve", lambda e, k=k: e.tensor_scalar_mul(
                out_w[k].t[:], out_w[k].t[:], rmsw.t[:, k:k + 1]),
                reads=[rmsw, out_w[k]], writes=[out_w[k]])
        szr = [C.sbuf(pfx + f"szr{r}", [128, 16 * CS], BF16) for r in range(2)]
        xcr = [C.sbuf(pfx + f"xcr{r}", [128, 16 * CS], BF16) for r in range(2)]
        xtr = [[C.sbuf(pfx + f"xtr{r}_{half}", [128, cfg.DI], BF16) for half in range(2)]
               for r in range(2)]
        slr = [C.sbuf(pfx + f"slr{r}", [64, H * 64], F32) for r in range(2)]
        C.op("dve", lambda e: e.memset(slr[0].t[:], 0.0), writes=[slr[0]])
        sgjit = [C.sbuf(pfx + f"sgjit{i}", [64, 64], BF16) for i in range(2)]
        dcwTB = C.sbuf(pfx + "dcwTB", [128, 256], F32)
        pbslot = [Buf(pfx + f"pbs{s}") for s in range(4)]
        sm = [[C.sbuf(pfx + f"smB{i}_{j}", [128, 256], BF16) for j in range(2)]
              for i in range(2)]
        et = [C.sbuf(pfx + f"etB{j}", [128, 256], F32) for j in range(2)]
        mt = [[C.sbuf(pfx + f"mtB{i}_{j}", [128, 256], BF16) for j in range(2)] for i in range(4)]
        cet = [C.sbuf(pfx + f"cetB{i}", [64, 256], F32) for i in range(3)]
        cexp = [C.sbuf(pfx + f"cexpB{i}", [64, 256], BF16) for i in range(3)]
        ysq = [C.sbuf(pfx + f"ysq{i}", [128, CS], F32) for i in range(2)]
        ysb = [C.sbuf(pfx + f"ysb{i}", [128, 16 * CS], BF16) for i in range(2)]
        rstdB = C.sbuf(pfx + "rstdB", [1, CS], F32)
        onew = ysq

        def load_chunk(c, r):
            C.dma("sp", f"szl{r}", szr[r].t[:], sz_d[c, :, :], reads=[sz_db],
                  writes=[szr[r]])
            C.dma("sp", f"xcl{r}", xcr[r].t[:], xct_d[c, :, :], reads=[xct_db[c]],
                  writes=[xcr[r]])
            for half in range(2):
                C.dma("sp", f"xtl{r}{half}", xtr[r][half].t[:], xtn_d[c, half, :, :],
                      reads=[xtn_db[c]], writes=[xtr[r][half]])
            if c > 0:
                C.dma("sp", f"sll{r}", slr[r].t[:], s_d[c - 1, :, :],
                      reads=[s_db[c - 1]], writes=[slr[r]])

        load_chunk(0, 0)
        load_chunk(1, 1)
        for cp in range(NCH // 2):
            for c01 in range(2):
                c = 2 * cp + c01
                tc = c * CS
                for half in range(2):
                    for bi, band in ((0, dt_s), (1, nc_s)):
                        C.op("pe", lambda e, tc=tc, half=half, band=band: e.transpose(
                            PB[1].t[0:128, 0:32],
                            band.t[:, tc + half * 128:tc + half * 128 + 128],
                            identf.t[0:32, 0:32]),
                            reads=[band, identf], writes=[pbslot[0]])
                        sgn = 1.0 if bi == 0 else -1.0
                        C.op("dve", lambda e, c01=c01, half=half, bi=bi, sgn=sgn:
                             e.tensor_scalar_mul(
                                dcwTB.t[:, c01 * 128 + half * 64 + bi * 32:
                                        c01 * 128 + half * 64 + bi * 32 + 32],
                                PB[1].t[:, 0:32], sgn),
                             reads=[pbslot[0]], writes=[dcwTB])
                for half in range(2):
                    C.op("pe", lambda e, tc=tc, c01=c01, half=half: e.matmul(
                        PB[0].t[:, c01 * 256 if False else (half * 256):(half + 1) * 256]
                        if False else PB[0].t[:, half * 256:(half + 1) * 256],
                        b_ct.t[:, tc + half * 128:tc + half * 128 + 128],
                        c0_ct.t[:, tc:tc + CS],
                        start=True, stop=True),
                        reads=[b_ct, c0_ct], writes=[PB[0]])
                    C.op("dve", lambda e, c01=c01, half=half: e.tensor_mul(
                        sm[c01][half].t[:], PB[0].t[:, half * 256:(half + 1) * 256],
                        mask[half].t[:]),
                        reads=[PB[0], mask[half]], writes=[sm[c01][half]])
            tcp = 2 * cp * CS

            def emit_bcast(h):
                pbc = PB[1 + h % 3]
                C.op("pe", lambda e, h=h, tcp=tcp, pbc=pbc: e.matmul(
                    pbc.t[:, 0:512], onehot.t[:, h * 128:(h + 1) * 128],
                    nc_s.t[:, tcp:tcp + 512], start=True, stop=True),
                    reads=[onehot, nc_s], writes=[pbc])
            emit_bcast(0)
            for h in range(H):
                hr2 = h % 2
                pbc = PB[1 + h % 3]
                if h + 1 < H:
                    emit_bcast(h + 1)
                for c01 in range(2):
                    c = 2 * cp + c01
                    tc = c * CS
                    r = c % 2
                    hq = hr2 * 2 + c01
                    hq3 = (h * 2 + c01) % 3
                    ybank = PB[4 + c01 * 2 + ((h // 2) % 2)]
                    for half in range(2):
                        eb = et[half]
                        mb = mt[hq][half]
                        C.op("act", lambda e, h=h, c01=c01, half=half, pbc=pbc, eb=eb:
                             e.activation(
                                out=eb.t[:], in_=pbc.t[:, c01 * 256:c01 * 256 + 256],
                                func=AF.Relu, scale=1.0,
                                bias=dcwTB.t[:, c01 * 128 + half * 64 + 32 + h:
                                             c01 * 128 + half * 64 + 32 + h + 1]),
                             reads=[pbc, dcwTB], writes=[eb])
                        C.op("act", lambda e, eb=eb, mb=mb: e.activation(
                            out=mb.t[:], in_=eb.t[:], func=AF.Exp, scale=-1.0),
                            reads=[eb], writes=[mb])
                        C.op("dve", lambda e, h=h, c01=c01, half=half, mb=mb:
                             e.scalar_tensor_tensor(
                                out=mb.t[:], in0=mb.t[:],
                                scalar=dcwTB.t[:, c01 * 128 + half * 64 + h:
                                               c01 * 128 + half * 64 + h + 1],
                                in1=sm[c01][half].t[:], op0=ALU.mult, op1=ALU.mult),
                             reads=[mb, dcwTB, sm[c01][half]], writes=[mb])
                    sgj = sgjit[c01]
                    slprev = slr[c % 2]
                    if s_in is not None:
                        C.op("dve", lambda e, c=c, h=h, slprev=slprev, sgj=sgj:
                             e.scalar_tensor_tensor(
                                out=sgj.t[:], in0=s_in.t[:, h * 64:(h + 1) * 64],
                                scalar=eprefT.t[:, h * 16 + c:h * 16 + c + 1],
                                in1=slprev.t[:, h * 64:(h + 1) * 64],
                                op0=ALU.mult, op1=ALU.add),
                             reads=[s_in, eprefT, slprev], writes=[sgj])
                    else:
                        C.op("dve", lambda e, h=h, slprev=slprev, sgj=sgj: e.tensor_copy(
                            sgj.t[:], slprev.t[:, h * 64:(h + 1) * 64]),
                            reads=[slprev], writes=[sgj])
                    cb = cet[hq3]
                    C.op("act", lambda e, h=h, c=c, c01=c01, pbc=pbc, cb=cb: e.activation(
                        out=cb.t[:], in_=pbc.t[0:64, c01 * 256:c01 * 256 + 256],
                        func=AF.Exp, scale=-1.0,
                        bias=nclprevT.t[0:64, h * 16 + c:h * 16 + c + 1]),
                        reads=[pbc, nclprevT], writes=[cb])
                    C.op("pool", lambda e, tc=tc, hq3=hq3, cb=cb: e.tensor_mul(
                        cexp[hq3].t[:], cb.t[:], c0_ct.t[:, tc:tc + CS]),
                        reads=[cb, c0_ct], writes=[cexp[hq3]])

                    ro = (h % 2) * 64
                    tp = None if ro == 0 else (0, 64)

                    def ygrp(e, h=h, r=r, hq=hq, hq3=hq3, c01=c01, ybank=ybank, ro=ro, tp=tp):
                        kw = {} if tp is None else {"tile_position": tp}
                        e.matmul(ybank.t[ro:ro + 64, 0:256],
                                 xtr[r][0].t[:, h * 64:(h + 1) * 64],
                                 mt[hq][0].t[:], start=True, stop=False, **kw)
                        e.matmul(ybank.t[ro:ro + 64, 0:256],
                                 xtr[r][1].t[:, h * 64:(h + 1) * 64],
                                 mt[hq][1].t[:], start=False, stop=False, **kw)
                        return e.matmul(ybank.t[ro:ro + 64, 0:256],
                                        sgjit[c01].t[:],
                                        cexp[hq3].t[:], start=False, stop=True, **kw)
                    C.group("pe", ygrp, reads=[xtr[r][0], xtr[r][1], mt[hq][0],
                                               mt[hq][1], sgjit[c01], cexp[hq3]],
                            writes=[ybank])
                    if ro == 64:
                        mtile = h // 2
                        C.op("dve", lambda e, r=r, c01=c01, mtile=mtile, ybank=ybank:
                             e.scalar_tensor_tensor(
                                out=ysb[c01].t[:, mtile * CS:(mtile + 1) * CS],
                                in0=xcr[r].t[:, mtile * CS:(mtile + 1) * CS],
                                scalar=dexp.t[:, mtile:mtile + 1],
                                in1=ybank.t[:, 0:256], op0=ALU.mult, op1=ALU.add),
                             reads=[xcr[r], dexp, ybank], writes=[ysb[c01]])
            for c01 in range(2):
                c = 2 * cp + c01
                tc = c * CS
                r = c % 2
                y_c = ysb[c01]
                for m in range(16):
                    C.op("dve", lambda e, m=m, r=r, y_c=y_c: e.tensor_mul(
                        y_c.t[:, m * CS:(m + 1) * CS], y_c.t[:, m * CS:(m + 1) * CS],
                        szr[r].t[:, m * CS:(m + 1) * CS]),
                        reads=[y_c, szr[r]], writes=[y_c])
                for m in range(16):
                    yq = ysq[m % 2]
                    C.op("act", lambda e, m=m, yq=yq, y_c=y_c: e.activation(
                        out=yq.t[:], in_=y_c.t[:, m * CS:(m + 1) * CS], func=AF.Square),
                        reads=[y_c], writes=[yq])
                    C.op("pe", lambda e, m=m, yq=yq: e.matmul(
                        PB[0].t[0:1, 0:256], onescol.t[:], yq.t[:],
                        start=(m == 0), stop=(m == 15)),
                        reads=[yq, onescol], writes=[PB[0]])
                C.op("act", lambda e: e.activation(out=rstdB.t[:], in_=PB[0].t[0:1, 0:256],
                                                   func=AF.Ln, scale=1.0 / cfg.DI,
                                                   bias=cstc.t[0:1, 0:1]),
                     reads=[PB[0], cstc], writes=[rstdB])
                C.op("act", lambda e: e.activation(out=rstdB.t[:], in_=rstdB.t[:],
                                                   func=AF.Exp, scale=-0.5),
                     reads=[rstdB], writes=[rstdB])
                C.op("pe", lambda e: e.matmul(PB[0].t[:, 0:256], ones1f.t[:], rstdB.t[:],
                                              start=True, stop=True),
                     reads=[rstdB, ones1f], writes=[PB[0]])
                for m in range(16):
                    C.op("dve", lambda e, m=m, y_c=y_c: e.tensor_mul(
                        y_c.t[:, m * CS:(m + 1) * CS], y_c.t[:, m * CS:(m + 1) * CS],
                        PB[0].t[:, 0:256]),
                        reads=[PB[0], y_c], writes=[y_c])
                for m in range(8):
                    ob = PB[4 + 2 * c01 + (m % 2)]

                    def ogrp(e, m=m, ob=ob, y_c=y_c):
                        last = None
                        for k in range(16):
                            last = e.matmul(ob.t[:, 0:256],
                                            out_w[k].t[:, m * 128:(m + 1) * 128],
                                            y_c.t[:, k * CS:(k + 1) * CS],
                                            start=(k == 0), stop=(k == 15))
                        return last
                    C.group("pe", ogrp, reads=out_w + [y_c], writes=[ob])
                    onw = onew[m % 2]
                    C.op("dve", lambda e, ob=ob, onw=onw: e.tensor_copy(onw.t[:], ob.t[:, 0:256]),
                         reads=[ob], writes=[onw])
                    C.dma("pool", f"hacc{m % 2}",
                          h_out[m * 128:(m + 1) * 128, tc:tc + CS], onw.t[:],
                          reads=[onw], writes=[hout_db[c][m]], accum_op=ALU.add)
            if 2 * cp + 2 < NCH:
                load_chunk(2 * cp + 2, 0)
            if 2 * cp + 3 < NCH:
                load_chunk(2 * cp + 3, 1)


def mamba_host_weights(in_w, conv_w, conv_b, dt_bias, A_log, Dp, rms_w, out_w, norm_w):
    import numpy as np
    import ml_dtypes
    f32 = np.float32
    out = {}
    out["w_inT"] = np.ascontiguousarray(in_w.T.astype(f32))
    out["out_wT"] = np.ascontiguousarray(out_w.T.astype(f32))
    out["conv_w"] = np.ascontiguousarray(
        conv_w.reshape(17, 128, 4).transpose(1, 0, 2).reshape(128, 68).astype(f32))
    out["conv_b"] = np.ascontiguousarray(conv_b.reshape(17, 128).T.astype(f32))
    out["dt_bias"] = dt_bias.reshape(32, 1).astype(f32)
    out["A_log"] = A_log.reshape(32, 1).astype(f32)
    out["Dexp"] = np.ascontiguousarray(
        np.repeat(Dp, 64).reshape(16, 128).T.astype(f32))
    out["rms_w"] = np.ascontiguousarray(rms_w.reshape(16, 128).T.astype(f32))
    out["norm_w"] = np.ascontiguousarray(norm_w.reshape(8, 128).T.astype(f32))
    oh = np.zeros((32, 32 * 128), f32)
    for h in range(32):
        oh[h, h * 128:(h + 1) * 128] = 1.0
    out["onehot"] = oh
    j = np.arange(128)[:, None]
    i = np.arange(256)[None, :]
    out["mask0"] = (i >= j).astype(ml_dtypes.bfloat16)
    out["mask1"] = (i >= j + 128).astype(ml_dtypes.bfloat16)
    out["ident"] = np.eye(128, dtype=ml_dtypes.bfloat16)
    out["identf"] = np.eye(128, dtype=f32)
    return out


def mamba_declare_weights(nc, pfx):
    import concourse.mybir as mybir
    F = mybir.dt.float32
    B = mybir.dt.bfloat16
    d = {}
    spec = {
        "w_inT": ([1024, 4256], F), "out_wT": ([2048, 1024], F),
        "conv_w": ([128, 68], F), "conv_b": ([128, 17], F),
        "dt_bias": ([32, 1], F), "A_log": ([32, 1], F),
        "Dexp": ([128, 16], F), "rms_w": ([128, 16], F), "norm_w": ([128, 8], F),
        "onehot": ([32, 32 * 128], F),
        "mask0": ([128, 256], B), "mask1": ([128, 256], B),
        "ident": ([128, 128], B), "identf": ([128, 128], F),
    }
    for nm, (shape, dt) in spec.items():
        d[nm] = nc.dram_tensor(pfx + nm, shape, dt, kind="ExternalInput").ap()
    return d


def mamba_declare_scratch(nc, cfg, pfx):
    import concourse.mybir as mybir
    F = mybir.dt.float32
    B = mybir.dt.bfloat16
    return {
        "sz_d": nc.dram_tensor(pfx + "sz_d", [cfg.NCH, 128, 16 * cfg.CS], B).ap(),
        "xbc_d": nc.dram_tensor(pfx + "xbc_d", [17, 128, cfg.T + 3], B).ap(),
        "xct_d": nc.dram_tensor(pfx + "xct_d", [cfg.NCH, 128, 16 * cfg.CS], B).ap(),
        "xtn_d": nc.dram_tensor(pfx + "xtn_d", [cfg.NCH, 2, 128, cfg.DI], B).ap(),
        "s_d": nc.dram_tensor(pfx + "s_d", [cfg.NCH, 64, cfg.H * 64], F).ap(),
    }


"""Top-level kernel assembly parts: LN pass, routing, FFN, collectives, kernel()."""



I32 = mybir.dt.int32


def emit_ln_pass(C, cfg, pfx, u_ap, h2_ap, x_out, ln_out, halo_in, halo_out, add_res, ln_f32=False):
    T, D, KD = cfg.T, cfg.D, cfg.KD
    with C.phase(pfx + "LN"):
        PP = [C.psum(pfx + f"lnp{i}", [128, 512], F32) for i in range(2)]
        ub = [[C.sbuf(pfx + f"lnu{r}_{k}", [128, 512], F32) for k in range(KD)] for r in range(2)]
        hb = [[C.sbuf(pfx + f"lnh{r}_{k}", [128, 512], F32) for k in range(KD)] for r in range(2)]
        xb = [[C.sbuf(pfx + f"lnx{r}_{k}", [128, 512], F32) for k in range(KD)] for r in range(2)]
        sq = [C.sbuf(pfx + f"lnsq{k}", [128, 512], F32) for k in range(KD)]
        lb = [C.sbuf(pfx + f"lnl{k}", [128, 512], F32 if ln_f32 else BF16) for k in range(KD)]
        mrow = C.sbuf(pfx + "lnm", [1, 512], F32)
        rrow = C.sbuf(pfx + "lnr", [1, 512], F32)
        onescol = C.sbuf(pfx + "lnoc", [128, 1], F32)
        ones1f = C.sbuf(pfx + "lno1", [1, 128], F32)
        cstc = C.sbuf(pfx + "lncst", [128, 1], F32)
        C.op("dve", lambda e: e.memset(onescol.t[:], 1.0 / D), writes=[onescol])
        C.op("dve", lambda e: e.memset(ones1f.t[:], 1.0), writes=[ones1f])
        C.op("dve", lambda e: e.memset(cstc.t[:], cfg.EPS), writes=[cstc])
        C.act_table(6)

        ntg = T // 512
        for g in range(ntg + (1 if halo_in is not None else 0)):
            halo_g = (g == ntg)
            r = g % 2
            t0 = g * 512
            N = 3 if halo_g else 512
            for k in range(KD):
                src = halo_in if halo_g else u_ap
                c0, c1 = (0, 3) if halo_g else (t0, t0 + 512)
                C.dma("sp", f"lnul{r}", ub[r][k].t[:, 0:N],
                      src[k * 128:(k + 1) * 128, c0:c1], writes=[ub[r][k]])
            C.fence_stream(f"lnul{r}", written=ub[r])
            if add_res and not halo_g:
                for k in range(KD):
                    C.dma("sp", f"lnhl{r}", hb[r][k].t[:, 0:N],
                          h2_ap[k * 128:(k + 1) * 128, t0:t0 + 512], writes=[hb[r][k]])
                C.fence_stream(f"lnhl{r}", written=hb[r])
                for k in range(KD):
                    C.op("dve", lambda e, k=k, r=r: e.tensor_add(
                        xb[r][k].t[:], ub[r][k].t[:], hb[r][k].t[:]),
                        reads=[ub[r][k], hb[r][k]], writes=[xb[r][k]])
                xcur = xb[r]
            else:
                xcur = ub[r]
            for k in range(KD):
                C.op("pe", lambda e, k=k, xc=xcur, N=N: e.matmul(
                    PP[0].t[0:1, 0:N], onescol.t[:], xc[k].t[:, 0:N],
                    start=(k == 0), stop=(k == KD - 1)),
                    reads=[xcur[k], onescol], writes=[PP[0]])
            C.op("dve", lambda e, N=N: e.tensor_copy(mrow.t[:, 0:N], PP[0].t[0:1, 0:N]),
                 reads=[PP[0]], writes=[mrow])
            for k in range(KD):
                C.op("act", lambda e, k=k, xc=xcur, N=N: e.activation(
                    out=sq[k].t[:, 0:N], in_=xc[k].t[:, 0:N], func=AF.Square),
                    reads=[xcur[k]], writes=[sq[k]])
                C.op("pe", lambda e, k=k, N=N: e.matmul(
                    PP[0].t[0:1, 0:N], onescol.t[:], sq[k].t[:, 0:N],
                    start=(k == 0), stop=(k == KD - 1)),
                    reads=[sq[k], onescol], writes=[PP[0]])
            C.op("dve", lambda e, N=N: e.scalar_tensor_tensor(
                out=rrow.t[:, 0:N], in0=mrow.t[:, 0:N], scalar=-1.0, in1=mrow.t[:, 0:N],
                op0=ALU.mult, op1=ALU.mult),
                reads=[mrow], writes=[rrow])
            C.op("dve", lambda e, N=N: e.tensor_add(
                rrow.t[:, 0:N], rrow.t[:, 0:N], PP[0].t[0:1, 0:N]),
                reads=[rrow, PP[0]], writes=[rrow])
            C.op("act", lambda e, N=N: e.activation(
                out=rrow.t[:, 0:N], in_=rrow.t[:, 0:N], func=AF.Ln, bias=cstc.t[0:1, :]),
                reads=[rrow, cstc], writes=[rrow])
            C.op("act", lambda e, N=N: e.activation(
                out=rrow.t[:, 0:N], in_=rrow.t[:, 0:N], func=AF.Exp, scale=-0.5),
                reads=[rrow], writes=[rrow])
            C.op("pe", lambda e, N=N: e.matmul(PP[0].t[:, 0:N], ones1f.t[:], mrow.t[:, 0:N],
                                               start=True, stop=True),
                 reads=[mrow, ones1f], writes=[PP[0]])
            C.op("pe", lambda e, N=N: e.matmul(PP[1].t[:, 0:N], ones1f.t[:], rrow.t[:, 0:N],
                                               start=True, stop=True),
                 reads=[rrow, ones1f], writes=[PP[1]])
            for k in range(KD):
                C.op("dve", lambda e, k=k, xc=xcur, N=N: e.tensor_sub(
                    sq[k].t[:, 0:N], xc[k].t[:, 0:N], PP[0].t[:, 0:N]),
                    reads=[xcur[k], PP[0]], writes=[sq[k]])
                C.op("dve", lambda e, k=k, N=N: e.tensor_mul(
                    lb[k].t[:, 0:N], sq[k].t[:, 0:N], PP[1].t[:, 0:N]),
                    reads=[sq[k], PP[1]], writes=[lb[k]])
            for k in range(KD):
                if halo_g:
                    C.op("dve", lambda e, k=k: e.tensor_mul(
                        sq[k].t[:, 0:3], sq[k].t[:, 0:3], PP[1].t[:, 0:3]),
                        reads=[sq[k], PP[1]], writes=[sq[k]])
                    C.dma("sp", "lnho", halo_out[k * 128:(k + 1) * 128, :],
                          sq[k].t[:, 0:3], reads=[sq[k]])
                else:
                    C.dma("sp", f"lnlo{r}", ln_out[k * 128:(k + 1) * 128, t0:t0 + 512],
                          lb[k].t[:, 0:N], reads=[lb[k]])
                    if x_out is not None:
                        C.dma("sp", f"lnxo{r}", x_out[k * 128:(k + 1) * 128, t0:t0 + 512],
                              xcur[k].t[:, 0:N], reads=[xcur[k]])
            if not halo_g:
                for k in range(KD):
                    lb[k].readers.append((f"lnlo{r}", C.cnt[f"lnlo{r}"]))
                    if x_out is not None:
                        xcur[k].readers.append((f"lnxo{r}", C.cnt[f"lnxo{r}"]))


def emit_routing(C, cfg, pfx, W, ln_d, rout_d):
    T, D, KD = cfg.T, cfg.D, cfg.KD
    NW = 4
    with C.phase(pfx + "RT"):
        PP = [C.psum(pfx + f"rtp{i}", [128, 512], F32) for i in range(NW)]
        gw = C.sbuf(pfx + "rtgw", [128, 8 * KD], BF16)
        gb = C.sbuf(pfx + "rtgb", [8, 1], F32)
        lnb = [C.sbuf(pfx + f"rtln{k}", [128, 512], BF16) for k in range(KD)]
        lg = C.sbuf(pfx + "rtlg", [8, T], F32)
        lgo = C.sbuf(pfx + "rtlgo", [8, T], F32)
        lgT = [C.sbuf(pfx + f"rtlgT{i}", [128, 8], F32) for i in range(NW)]
        m1 = [C.sbuf(pfx + f"rtm1{i}", [128, 1], F32) for i in range(NW)]
        m2 = [C.sbuf(pfx + f"rtm2{i}", [128, 1], F32) for i in range(NW)]
        w1 = [C.sbuf(pfx + f"rtw1{i}", [128, 1], F32) for i in range(NW)]
        w2 = [C.sbuf(pfx + f"rtw2{i}", [128, 1], F32) for i in range(NW)]
        eq = [C.sbuf(pfx + f"rteq{i}", [128, 8], F32) for i in range(NW)]
        cumt = [C.sbuf(pfx + f"rtcum{i}", [128, 8], F32) for i in range(NW)]
        fst = [[C.sbuf(pfx + f"rtf{i}_{j}", [128, 8], F32) for j in range(2)]
               for i in range(NW)]
        l2 = [C.sbuf(pfx + f"rtl2{i}", [128, 8], F32) for i in range(NW)]
        rt = [C.sbuf(pfx + f"rtr{i}", [128, 8], F32) for i in range(NW)]
        zer8 = C.sbuf(pfx + "rtz", [128, 8], F32)
        identf = C.sbuf(pfx + "rtid", [128, 128], F32)
        C.dma("pool", "rtc2", gw.t[:], W["gate_wT"][:, :], writes=[gw])
        C.dma("sp", "rtc", gb.t[:], W["gate_b"][:, :], writes=[gb])
        C.dma("sp", "rtc", identf.t[:], W["identf2"][:, :], writes=[identf])
        C.fence_stream("rtc", written=[gb, identf])
        C.fence_stream("rtc2", written=[gw])
        C.op("dve", lambda e: e.memset(zer8.t[:], 0.0), writes=[zer8])
        C.act_table(2)

        for g in range(T // 512):
            t0 = g * 512
            r = g % 2
            for k in range(KD):
                C.dma("sp", f"rtll{r}", lnb[k].t[:], ln_d[k * 128:(k + 1) * 128, t0:t0 + 512],
                      writes=[lnb[k]])
            C.fence_stream(f"rtll{r}", written=lnb)

            def ggrp(e, t0=t0):
                last = None
                for k in range(KD):
                    last = e.matmul(PP[0].t[0:8, 0:512],
                                    gw.t[:, k * 8:(k + 1) * 8], lnb[k].t[:],
                                    start=(k == 0), stop=(k == KD - 1))
                return last
            C.group("pe", ggrp, reads=[gw] + lnb, writes=[PP[0]])
            C.op("dve", lambda e, t0=t0: e.tensor_scalar_add(
                lg.t[:, t0:t0 + 512], PP[0].t[0:8, 0:512], gb.t[:]),
                reads=[PP[0], gb], writes=[lg])
        def tiles_of(i0):
            return [(i0 + j, j) for j in range(NW) if i0 + j < T // 128]

        for i0 in range(0, T // 128, NW):
            tl = tiles_of(i0)
            for i, r in tl:
                C.op("pe", lambda e, i=i, pb=PP[r]: e.transpose(
                    pb.t[0:128, 0:8], lg.t[:, i * 128:(i + 1) * 128], identf.t[0:8, 0:8]),
                    reads=[lg, identf], writes=[PP[r]])
            for i, r in tl:
                C.op("dve", lambda e, lt=lgT[r], pb=PP[r]: e.tensor_copy(lt.t[:], pb.t[:, 0:8]),
                     reads=[PP[r]], writes=[lgT[r]])
            for i, r in tl:
                C.op("dve", lambda e, lt=lgT[r], m_1=m1[r]: e.reduce_max(
                    m_1.t[:], lt.t[:], axis=AX.X), reads=[lgT[r]], writes=[m1[r]])
            for i, r in tl:
                C.op("dve", lambda e, lt=lgT[r], e_q=eq[r], m_1=m1[r]: e.tensor_single_scalar(
                    e_q.t[:], lt.t[:], m_1.t[:], ALU.is_equal),
                    reads=[lgT[r], m1[r]], writes=[eq[r]])
            for i, r in tl:
                C.op("dve", lambda e, e_q=eq[r], cu=cumt[r]: e.tensor_tensor_scan(
                    out=cu.t[:], data0=e_q.t[:], data1=zer8.t[:], initial=0.0,
                    op0=ALU.add, op1=ALU.add), reads=[eq[r], zer8], writes=[cumt[r]])
            for i, r in tl:
                C.op("dve", lambda e, cu=cumt[r]: e.tensor_single_scalar(
                    cu.t[:], cu.t[:], 1.0, ALU.is_equal), reads=[cumt[r]], writes=[cumt[r]])
            for i, r in tl:
                C.op("dve", lambda e, e_q=eq[r], cu=cumt[r], f_s=fst[r]: e.tensor_mul(
                    f_s[0].t[:], e_q.t[:], cu.t[:]), reads=[eq[r], cumt[r]], writes=[fst[r][0]])
            for i, r in tl:
                C.op("dve", lambda e, lt=lgT[r], f_s=fst[r], l_2=l2[r]: e.scalar_tensor_tensor(
                    out=l_2.t[:], in0=f_s[0].t[:], scalar=-1e30, in1=lt.t[:],
                    op0=ALU.mult, op1=ALU.add), reads=[fst[r][0], lgT[r]], writes=[l2[r]])
            for i, r in tl:
                C.op("dve", lambda e, l_2=l2[r], m_2=m2[r]: e.reduce_max(
                    m_2.t[:], l_2.t[:], axis=AX.X), reads=[l2[r]], writes=[m2[r]])
            for i, r in tl:
                C.op("dve", lambda e, l_2=l2[r], m_2=m2[r], e_q=eq[r]: e.tensor_single_scalar(
                    e_q.t[:], l_2.t[:], m_2.t[:], ALU.is_equal),
                    reads=[l2[r], m2[r]], writes=[eq[r]])
            for i, r in tl:
                C.op("dve", lambda e, e_q=eq[r], cu=cumt[r]: e.tensor_tensor_scan(
                    out=cu.t[:], data0=e_q.t[:], data1=zer8.t[:], initial=0.0,
                    op0=ALU.add, op1=ALU.add), reads=[eq[r], zer8], writes=[cumt[r]])
            for i, r in tl:
                C.op("dve", lambda e, cu=cumt[r]: e.tensor_single_scalar(
                    cu.t[:], cu.t[:], 1.0, ALU.is_equal), reads=[cumt[r]], writes=[cumt[r]])
            for i, r in tl:
                C.op("dve", lambda e, e_q=eq[r], cu=cumt[r], f_s=fst[r]: e.tensor_mul(
                    f_s[1].t[:], e_q.t[:], cu.t[:]), reads=[eq[r], cumt[r]], writes=[fst[r][1]])
            for i, r in tl:
                C.op("dve", lambda e, m_1=m1[r], m_2=m2[r], w_1=w1[r]: e.tensor_sub(
                    w_1.t[:], m_1.t[:], m_2.t[:]), reads=[m1[r], m2[r]], writes=[w1[r]])
            for i, r in tl:
                C.op("act", lambda e, w_1=w1[r], w_2=w2[r]: e.activation(
                    out=w_2.t[:], in_=w_1.t[:], func=AF.Sigmoid, scale=-1.0),
                    reads=[w1[r]], writes=[w2[r]])
            for i, r in tl:
                C.op("act", lambda e, w_1=w1[r]: e.activation(
                    out=w_1.t[:], in_=w_1.t[:], func=AF.Sigmoid),
                    reads=[w1[r]], writes=[w1[r]])
            for i, r in tl:
                C.op("dve", lambda e, ro=rt[r], f_s=fst[r], w_1=w1[r]: e.tensor_scalar_mul(
                    ro.t[:], f_s[0].t[:], w_1.t[:]), reads=[fst[r][0], w1[r]], writes=[rt[r]])
            for i, r in tl:
                C.op("dve", lambda e, ro=rt[r], f_s=fst[r], w_2=w2[r]: e.scalar_tensor_tensor(
                    out=ro.t[:], in0=f_s[1].t[:], scalar=w_2.t[:], in1=ro.t[:],
                    op0=ALU.mult, op1=ALU.add), reads=[fst[r][1], w2[r], rt[r]], writes=[rt[r]])
            for i, r in tl:
                C.op("pe", lambda e, i=i, ro=rt[r], pb=PP[r]: e.transpose(
                    pb.t[0:8, 0:128], ro.t[:], identf.t[:]),
                    reads=[rt[r], identf], writes=[PP[r]])
            for i, r in tl:
                C.op("dve", lambda e, i=i, pb=PP[r]: e.tensor_copy(
                    lgo.t[:, i * 128:(i + 1) * 128], pb.t[0:8, 0:128]),
                    reads=[PP[r]], writes=[lgo])
        for q in range(2):
            C.dma("sp", "rto", rout_d[:, q * (T // 2):(q + 1) * (T // 2)],
                  lgo.t[0:8, q * (T // 2):(q + 1) * (T // 2)], reads=[lgo])
        C.fence_stream("rto", written=[])


def emit_ffn_prep(C, cfg, pfx, W):
    MF = 32
    w1sb = C.sbuf(pfx + "w1sb", [128, MF * 1024], BF16)
    w2sb = C.sbuf(pfx + "w2sb", [128, MF * 1024], BF16)
    b1sb = C.sbuf(pfx + "ffb1", [128, 8 * MF], F32)
    b2sb = C.sbuf(pfx + "ffb2c", [128, 64], F32)
    C.dma("sp", "ffc", b1sb.t[:], W["b1h"][:, :], writes=[b1sb])
    C.dma("sp", "ffc", b2sb.t[:], W["b2c"][:, :], writes=[b2sb])
    C.fence_stream("ffc", written=[b1sb, b2sb])

    def load_w1(e_i):
        for g in range(8):
            mf = 4 * g
            src = W["w1t"][e_i * MF + mf:e_i * MF + mf + 4, :, :].rearrange(
                "m p c -> p m c")
            dst = w1sb.t[:, mf * 1024:(mf + 4) * 1024].rearrange(
                "p (m c) -> p m c", m=4)
            C.dma("pool", "fw1", dst, src, writes=[w1sb])
        C.fence_stream("fw1", written=[w1sb])

    def load_w2(e_i):
        for mfq in range(8):
            src = W["w2"][e_i].rearrange("(q m p) c -> q m p c", q=8, p=128)[mfq]
            src = src.rearrange("m p c -> p m c")
            dst = w2sb.t[:, mfq * 4096:(mfq + 1) * 4096].rearrange(
                "p (m c) -> p m c", m=4)
            C.dma("pool", "fw2", dst, src, writes=[w2sb])
        C.fence_stream("fw2", written=[w2sb])

    load_w1(0)
    load_w2(0)
    return {"w1sb": w1sb, "w2sb": w2sb, "b1sb": b1sb, "b2sb": b2sb,
            "load_w1": load_w1, "load_w2": load_w2}


def emit_ffn(C, cfg, pfx, W, ln_d, rout_d, out_ext, out_db, st):
    T, D = cfg.T, cfg.D
    MF = 32
    NT = T // 512
    w1sb, w2sb = st["w1sb"], st["w2sb"]
    b1sb, b2sb = st["b1sb"], st["b2sb"]
    with C.phase(pfx + "FF"):
        PS = [C.psum(pfx + f"ffp{i}", [128, 512], F32) for i in range(8)]
        h_sb = C.sbuf(pfx + "ffh", [128, MF * 512], BF16)
        h_t = [Buf(pfx + f"ffh{mf}", h_sb.t) for mf in range(MF)]
        lnr = [C.sbuf(pfx + f"fflnr{r}", [128, 8 * 512], BF16) for r in range(2)]
        wb = [C.sbuf(pfx + f"ffwbr{r}", [128, 512], F32) for r in range(2)]
        wer = [C.sbuf(pfx + f"ffwe{r}", [1, 512], F32) for r in range(2)]
        tmp = [C.sbuf(pfx + f"fftm{i}", [128, 512], F32) for i in range(4)]
        ones1f = C.sbuf(pfx + "ffo1", [1, 128], F32)
        C.op("dve", lambda e: e.memset(ones1f.t[:], 1.0), writes=[ones1f])
        C.act_table(10)

        def load_ln(e_i, tb, r):
            t0 = tb * 512
            src = ln_d[:, t0:t0 + 512].rearrange("(k p) t -> p k t", k=8)
            dst = lnr[r].t[:].rearrange("p (k t) -> p k t", k=8)
            C.dma("sp", f"flnl{r}", dst, src, writes=[lnr[r]])
            C.dma("sp", f"fwer{r}", wer[r].t[:], rout_d[e_i:e_i + 1, t0:t0 + 512],
                  writes=[wer[r]])

        steps = [(e_i, tb) for e_i in range(8) for tb in range(NT)]
        load_ln(0, 0, 0)
        for si, (e_i, tb) in enumerate(steps):
            r = si % 2
            t0 = tb * 512
            if si + 1 < len(steps):
                load_ln(steps[si + 1][0], steps[si + 1][1], 1 - r)
            C.op("pe", lambda e, r=r: e.matmul(
                PS[2].t[:, 0:512], ones1f.t[:], wer[r].t[:], start=True, stop=True),
                reads=[ones1f, wer[r]], writes=[PS[2]])
            C.op("dve", lambda e, r=r: e.tensor_copy(wb[r].t[:], PS[2].t[:, 0:512]),
                 reads=[PS[2]], writes=[wb[r]])
            for mfp in range(MF // 2):
                bks = (0, 1) if mfp % 2 == 0 else (6, 7)

                def w1grp(e, mfp=mfp, r=r, bks=bks):
                    last = None
                    for j in range(2):
                        mf = 2 * mfp + j
                        for k in range(8):
                            last = e.matmul(
                                PS[bks[j]].t[:, 0:512],
                                w1sb.t[:, mf * 1024 + k * 128:mf * 1024 + (k + 1) * 128],
                                lnr[r].t[:, k * 512:(k + 1) * 512],
                                start=(k == 0), stop=(k == 7))
                    return last
                C.group("pe", w1grp, reads=[w1sb, lnr[r]],
                        writes=[PS[bks[0]], PS[bks[1]]])
                for j in range(2):
                    mf = 2 * mfp + j
                    C.op("act", lambda e, mf=mf, e_i=e_i, b=bks[j]: e.activation(
                        out=h_sb.t[:, mf * 512:(mf + 1) * 512], in_=PS[b].t[:, 0:512],
                        func=AF.Gelu, bias=b1sb.t[:, e_i * MF + mf:e_i * MF + mf + 1]),
                        reads=[PS[bks[j]], b1sb], writes=[h_t[mf]])
            if tb == NT - 1 and e_i + 1 < 8:
                st["load_w1"](e_i + 1)
            for dp in range(4):
                bk = (2, 3) if dp % 2 == 0 else (4, 5)
                for mfg in range(4):

                    def w2grp(e, dp=dp, mfg=mfg, bk=bk):
                        last = None
                        for mfl in range(8):
                            mf = mfg * 8 + mfl
                            for dl in range(2):
                                d = dp * 2 + dl
                                last = e.matmul(
                                    PS[bk[dl]].t[:, 0:512],
                                    w2sb.t[:, mf * 1024 + d * 128:mf * 1024 + (d + 1) * 128],
                                    h_sb.t[:, mf * 512:(mf + 1) * 512],
                                    start=(mf == 0), stop=(mf == MF - 1))
                        return last
                    C.group("pe", w2grp,
                            reads=[w2sb] + [h_t[mfg * 8 + i] for i in range(8)],
                            writes=[PS[bk[0]], PS[bk[1]]])
                for dl in range(2):
                    d = dp * 2 + dl
                    tp = tmp[(dp * 2 + dl) % 4]
                    C.op("dve", lambda e, d=d, e_i=e_i, r=r, tp=tp, b=bk[dl]:
                         e.scalar_tensor_tensor(
                            out=tp.t[:], in0=PS[b].t[:, 0:512],
                            scalar=b2sb.t[:, e_i * 8 + d:e_i * 8 + d + 1],
                            in1=wb[r].t[:], op0=ALU.add, op1=ALU.mult),
                        reads=[PS[bk[dl]], b2sb, wb[r]], writes=[tp])
                    C.dma("pool", f"ffac{dl}",
                          out_ext[d * 128:(d + 1) * 128, t0:t0 + 512],
                          tp.t[:], reads=[tp], writes=[out_db[d * NT + tb]],
                          accum_op=ALU.add)
            if tb == NT - 1 and e_i + 1 < 8:
                st["load_w2"](e_i + 1)


"""Top-level: build the full per-core program and kernel() entry."""


def build_program(cfg):
    nc = bass.Bass("TRN2", target_bir_lowering=False)
    T, D = cfg.T, cfg.D
    F32_ = F32

    u_in = nc.dram_tensor("u_in", [D, T], F32_, kind="ExternalInput").ap()
    u_halo = nc.dram_tensor("u_halo", [D, 3], F32_, kind="ExternalInput").ap()
    out_ext = nc.dram_tensor("out", [D, T], F32_, kind="ExternalOutput").ap()

    WM = [mamba_declare_weights(nc, f"Lm{l}_") for l in range(2)]
    Wffn = {
        "gate_wT": nc.dram_tensor("gate_wT", [128, 64], F32_, kind="ExternalInput").ap(),
        "gate_b": nc.dram_tensor("gate_b", [8, 1], F32_, kind="ExternalInput").ap(),
        "identf2": nc.dram_tensor("identf2", [128, 128], F32_, kind="ExternalInput").ap(),
        "w1t": nc.dram_tensor("ffn_w1t", [8 * 32, 128, 1024], F32_, kind="ExternalInput").ap(),
        "w2": nc.dram_tensor("ffn_w2", [8, 4096, 1024], F32_, kind="ExternalInput").ap(),
        "b1h": nc.dram_tensor("ffn_b1h", [128, 8 * 32], F32_, kind="ExternalInput").ap(),
        "b2c": nc.dram_tensor("ffn_b2c", [128, 64], F32_, kind="ExternalInput").ap(),
    }
    amask = nc.dram_tensor("amask", [32, 4], F32_, kind="ExternalInput").ap()
    bmask = nc.dram_tensor("bmask", [32, 4], F32_, kind="ExternalInput").ap()
    jmask = nc.dram_tensor("jmask", [32, 4], F32_, kind="ExternalInput").ap()
    hsel = nc.dram_tensor("hsel", [128, 4], F32_, kind="ExternalInput").ap()

    h0_d = nc.dram_tensor("h0_d", [D, T], F32_).ap()
    h1_d = nc.dram_tensor("h1_d", [D, T], F32_).ap()
    h2_d = nc.dram_tensor("h2_d", [D, T], F32_).ap()
    x_d = nc.dram_tensor("x_d", [D, T], F32_).ap()
    ln2_d = nc.dram_tensor("ln2_d", [D, T], BF16).ap()
    halo0_d = nc.dram_tensor("halo0_d", [D, 3], F32_).ap()
    halo1_d = nc.dram_tensor("halo1_d", [D, 3], F32_).ap()
    rout_d = nc.dram_tensor("rout_d", [8, T], F32_).ap()
    cc_s_in = [nc.dram_tensor(f"cc_s_in{l}", [65, 2048], F32_).ap() for l in range(2)]
    cc_s_out = [nc.dram_tensor(f"cc_s_out{l}", [260, 2048], F32_).ap() for l in range(2)]
    cc_h_in = nc.dram_tensor("cc_h_in", [D, 3], F32_).ap()
    cc_h_out = nc.dram_tensor("cc_h_out", [4 * D, 3], F32_).ap()
    wm_d = [nc.dram_tensor(f"wm_d{l}", [32, 4], F32_).ap() for l in range(2)]

    groups = [[0, 1, 2, 3], [4, 5, 6, 7]]

    with Ctx(nc) as C:
        def make_sin_provider(l, cc_in_db):
            def provider(C):
                cc_out_db = Buf(f"ccso{l}")
                C.op("pool", lambda e: e.collective_compute(
                    "AllGather", ALU.bypass, replica_groups=groups,
                    ins=[cc_s_in[l].opt()], outs=[cc_s_out[l].opt()]),
                    reads=[cc_in_db], writes=[cc_out_db])
                s_in = C.sbuf(f"s_in{l}", [64, 2048], F32)
                with C.phase(f"sinc{l}"):
                    pcmb = C.psum(f"sinp{l}", [128, 512], F32)
                    decj = [C.sbuf(f"sind{l}_{j}", [32, 1], F32) for j in range(4)]
                    am = C.sbuf(f"sinam{l}", [32, 4], F32)
                    bm = C.sbuf(f"sinbm{l}", [32, 4], F32)
                    jm = C.sbuf(f"sinjm{l}", [32, 4], F32)
                    eff = [C.sbuf(f"sine{l}_{j}", [32, 1], F32) for j in range(4)]
                    run = [C.sbuf(f"sinr{l}_{j}", [32, 1], F32) for j in range(4)]
                    wmat = C.sbuf(f"sinw{l}", [32, 4], F32)
                    wrow = C.sbuf(f"sinwr{l}", [1, 128], F32)
                    wbc = C.sbuf(f"sinwb{l}", [64, 128], F32)
                    o1 = C.sbuf(f"sino{l}", [1, 128], F32)
                    sjt = [C.sbuf(f"sins{l}_{i}", [64, 2048], F32) for i in range(2)]
                    C.op("dve", lambda e: e.memset(o1.t[:], 1.0), writes=[o1])
                    C.dma("sp", "sic", am.t[:], amask[:, :], writes=[am])
                    C.dma("sp", "sic", bm.t[:], bmask[:, :], writes=[bm])
                    C.dma("sp", "sic", jm.t[:], jmask[:, :], writes=[jm])
                    C.fence_stream("sic", written=[am, bm, jm])
                    for j in range(4):
                        src = cc_s_out[l][j * 65 + 64:j * 65 + 65, 0:32].rearrange("o h -> h o")
                        C.dma("sp", f"sid{j}", decj[j].t[:], src,
                              reads=[cc_out_db], writes=[decj[j]])
                        C.op("dve", lambda e, j=j: e.tensor_scalar(
                            out=eff[j].t[:], in0=decj[j].t[:], scalar1=am.t[:, j:j + 1],
                            scalar2=None, op0=ALU.mult),
                            reads=[decj[j], am], writes=[eff[j]])
                        C.op("dve", lambda e, j=j: e.tensor_scalar_add(
                            eff[j].t[:], eff[j].t[:], bm.t[:, j:j + 1]),
                            reads=[eff[j], bm], writes=[eff[j]])
                    C.op("dve", lambda e: e.memset(run[3].t[:], 1.0), writes=[run[3]])
                    for j in (2, 1, 0):
                        C.op("dve", lambda e, j=j: e.tensor_mul(
                            run[j].t[:], run[j + 1].t[:], eff[j + 1].t[:]),
                            reads=[run[j + 1], eff[j + 1]], writes=[run[j]])
                    for j in range(4):
                        C.op("dve", lambda e, j=j: e.tensor_scalar_mul(
                            wmat.t[:, j:j + 1], run[j].t[:], jm.t[:, j:j + 1]),
                            reads=[run[j], jm], writes=[wmat])
                    C.dma("sp", "siw", wm_d[l][:, :], wmat.t[:], reads=[wmat],
                          writes=[Buf(f"wmd{l}")])
                    wmd_b = Buf(f"wmd{l}")
                    wmd_b.last_w = {"siw": C.cnt["siw"]}
                    C.dma("sp", "siw2", wrow.t[:].rearrange("o (j h) -> o j h", j=4),
                          wm_d[l].rearrange("h j -> () j h"), reads=[wmd_b], writes=[wrow],
                          allow_slow_non_contiguous=True)
                    C.op("pe", lambda e: e.matmul(pcmb.t[:, 0:128], o1.t[0:1, 0:128],
                                                  wrow.t[:], start=True, stop=True),
                         reads=[o1, wrow], writes=[pcmb])
                    C.op("dve", lambda e: e.tensor_copy(wbc.t[:], pcmb.t[0:64, 0:128]),
                         reads=[pcmb], writes=[wbc])
                    C.op("dve", lambda e: e.memset(s_in.t[:], 0.0), writes=[s_in])
                    for j in range(4):
                        sj = sjt[j % 2]
                        C.dma("sp", f"sis{j % 2}", sj.t[:],
                              cc_s_out[l][j * 65:j * 65 + 64, :], reads=[cc_out_db],
                              writes=[sj])
                        for h in range(32):
                            C.op("dve", lambda e, j=j, h=h, sj=sj: e.scalar_tensor_tensor(
                                out=s_in.t[:, h * 64:(h + 1) * 64],
                                in0=sj.t[:, h * 64:(h + 1) * 64],
                                scalar=wbc.t[:, j * 32 + h:j * 32 + h + 1],
                                in1=s_in.t[:, h * 64:(h + 1) * 64],
                                op0=ALU.mult, op1=ALU.add),
                                reads=[sj, wbc, s_in], writes=[s_in])
                return s_in
            return provider

        emit_ln_pass(C, cfg, "N1", u_in, None, None, h0_d, u_halo, halo0_d, False,
                     ln_f32=True)

        with C.phase("L0w"):
            io0 = mamba_declare_scratch(nc, cfg, "Lm0_")
            cc_db0 = Buf("cc0db")
            io0.update(h_in=h0_d, h_out=h1_d, halo=halo0_d, cc_in=cc_s_in[0],
                       cc_db=cc_db0)
            emit_mamba_layer(C, cfg, "M0", WM[0], io0,
                             sin_provider=make_sin_provider(0, cc_db0))

        hx_st = {}
        halo1_db = Buf("halo1_db")

        def hx_pre(C):
            cc_h_db = Buf("cc_h_db")
            C.dma("sp", "hx0", cc_h_in[:, :], h1_d[:, T - 3:T], writes=[cc_h_db])
            hx_st.update(cc_h_db=cc_h_db)

        def hx_mid(C, g):
            if g == 2:
                cc_ho_db = Buf("cc_ho_db")
                C.op("pool", lambda e: e.collective_compute(
                    "AllGather", ALU.bypass, replica_groups=groups,
                    ins=[cc_h_in.opt()], outs=[cc_h_out.opt()]),
                    reads=[hx_st["cc_h_db"]], writes=[cc_ho_db])
                hx_st.update(cc_ho_db=cc_ho_db)
            if g != 5:
                return
            ht = C.sbuf("hxt", [128, 4], F32)
            hacc = C.sbuf("hxa", [128, 4], F32)
            hs = C.sbuf("hxs", [128, 4], F32)
            hselb = C.sbuf("hxsel", [128, 4], F32)
            cc_ho_db = hx_st["cc_ho_db"]
            C.dma("sp", "hx1", hselb.t[:], hsel[:, :], writes=[hselb])
            C.fence_stream("hx1", written=[hselb])
            for k in range(cfg.KD):
                C.op("dve", lambda e: e.memset(hacc.t[:], 0.0), writes=[hacc])
                for j in range(4):
                    C.dma("sp", "hx2", ht.t[:, 0:3],
                          cc_h_out[j * D + k * 128:j * D + (k + 1) * 128, :],
                          reads=[cc_ho_db], writes=[ht])
                    C.op("dve", lambda e, j=j: e.scalar_tensor_tensor(
                        out=hacc.t[:, 0:3], in0=ht.t[:, 0:3], scalar=hselb.t[:, j:j + 1],
                        in1=hacc.t[:, 0:3], op0=ALU.mult, op1=ALU.add),
                        reads=[ht, hselb, hacc], writes=[hacc])
                C.op("dve", lambda e: e.tensor_copy(hs.t[:, 0:3], hacc.t[:, 0:3]),
                     reads=[hacc], writes=[hs])
                C.dma("sp", "hx3", halo1_d[k * 128:(k + 1) * 128, :], hs.t[:, 0:3],
                      reads=[hs], writes=[halo1_db])

        with C.phase("L1w"):
            io1 = mamba_declare_scratch(nc, cfg, "Lm1_")
            cc_db1 = Buf("cc1db")
            io1.update(h_in=h1_d, h_out=h2_d, halo=halo1_d, cc_in=cc_s_in[1],
                       cc_db=cc_db1, halo_db=halo1_db)
            emit_mamba_layer(C, cfg, "M1", WM[1], io1,
                             sin_provider=make_sin_provider(1, cc_db1),
                             pre_hook=hx_pre, mid_hook=hx_mid)

        emit_ln_pass(C, cfg, "N2", u_in, h2_d, x_d, ln2_d, None, None, True)

        ffst = emit_ffn_prep(C, cfg, "FF", Wffn)

        emit_routing(C, cfg, "RT", Wffn, ln2_d, rout_d)

        out_db = [Buf(f"out_db{i}") for i in range(8 * (T // 512))]
        for q in range(4):
            C.dma("pool", "xcp", out_ext[:, q * (T // 4):(q + 1) * (T // 4)],
                  x_d[:, q * (T // 4):(q + 1) * (T // 4)], writes=out_db[:1])
        C.fence_stream("xcp", written=out_db)
        emit_ffn(C, cfg, "FF", Wffn, ln2_d, rout_d, out_ext, out_db, ffst)

        C.wait_all("sp", [(k, C.cnt[k]) for k in ("ffac0", "ffac1") if k in C.cnt])
    return nc


def host_prepare(inputs, core):
    b, q = core // 4, core % 4
    T = 4096
    u = inputs["L0_attn_u"][b]
    sh = np.ascontiguousarray(u[q * T:(q + 1) * T].T)
    if q == 0:
        uh = np.zeros((1024, 3), np.float32)
    else:
        uh = np.ascontiguousarray(u[q * T - 3:q * T].T)
    im = {"u_in": sh.astype(np.float32), "u_halo": uh.astype(np.float32)}
    for l in range(2):
        hw = mamba_host_weights(
            inputs["L0_attn_in_w"][l], inputs["L0_attn_conv_w"][l],
            inputs["L0_attn_conv_b"][l], inputs["L0_attn_dt_bias"][l],
            inputs["L0_attn_A_log"][l], inputs["L0_attn_Dp"][l],
            inputs["L0_attn_rms_w"][l], inputs["L0_attn_out_w"][l],
            inputs["L0_attn_norm_w"][l])
        for k, v in hw.items():
            im[f"Lm{l}_{k}"] = v
    gw = inputs["L0_ffn_gate_w"]
    im["gate_wT"] = np.ascontiguousarray(
        gw.reshape(8, 128, 8).transpose(1, 0, 2).reshape(128, 64)).astype(np.float32)
    im["gate_b"] = inputs["L0_ffn_gate_b"].reshape(8, 1).astype(np.float32)
    im["identf2"] = np.eye(128, dtype=np.float32)
    w1 = inputs["L0_ffn_w1"]
    im["ffn_w1t"] = np.ascontiguousarray(
        w1.reshape(8, 8, 128, 32, 128).transpose(0, 3, 2, 1, 4).reshape(8 * 32, 128, 8 * 128)
    ).astype(np.float32)
    im["ffn_w2"] = np.ascontiguousarray(inputs["L0_ffn_w2"]).astype(np.float32)
    b1 = inputs["L0_ffn_b1"]
    im["ffn_b1h"] = np.ascontiguousarray(
        b1.reshape(8, 32, 128).transpose(2, 0, 1).reshape(128, 256)).astype(np.float32)
    im["ffn_b2c"] = np.ascontiguousarray(
        inputs["L0_ffn_b2"].reshape(8, 8, 128).transpose(2, 0, 1).reshape(128, 64)
    ).astype(np.float32)
    r = q
    am = np.zeros((32, 4), np.float32)
    jm = np.zeros((32, 4), np.float32)
    for m in range(4):
        if m < r:
            am[:, m] = 1.0
    for j in range(4):
        if j < r:
            jm[:, j] = 1.0
    im["amask"] = am
    im["bmask"] = 1.0 - am
    im["jmask"] = jm
    hs = np.zeros((128, 4), np.float32)
    if r > 0:
        hs[:, r - 1] = 1.0
    im["hsel"] = hs
    return im


def kernel(**inputs):
    cfg = MCfg()
    nc = build_program(cfg)
    in_maps = [host_prepare(inputs, k) for k in range(8)]
    res = run_bass_kernel_spmd(nc, in_maps, core_ids=list(range(8)))
    out = np.empty((2, 16384, 1024), np.float32)
    for k in range(8):
        b, q = k // 4, k % 4
        out[b, q * 4096:(q + 1) * 4096, :] = res.results[k]["out"].T
    return out

